# version 08
# baseline (speedup 1.0000x reference)
import numpy as np
from contextlib import ExitStack

import concourse.bass as bass
import concourse.mybir as mybir
from concourse.bass_utils import run_bass_kernel_spmd

F32 = mybir.dt.float32
BF16 = mybir.dt.bfloat16
AF = mybir.ActivationFunctionType
ALU = mybir.AluOpType
AX = mybir.AxisListType

T = 2048
D = 2048
CC = 16
NH = 16
DH = 128
ROT = 32
HW = 1024
NEXP = 16
DI = 1408
NIC = 11
DIS = 5632
QSCALE = DH ** -0.5
EXPB = -4.0
NEG = -1e30

N_CORES = 8
TAPS = set()
STOP_AFTER = None
TAP_HEAD = 0
DEV_WAIT_CASTS = False
DEV_N_EXPERTS = None


def _host_consts():
    import ml_dtypes
    inv_freq = 10000.0 ** (-(np.arange(0, ROT, 2, dtype=np.float64) / ROT))
    tt = np.arange(T, dtype=np.float64)[:, None] * inv_freq[None, :]
    ang = np.concatenate([tt, tt], axis=1).T
    csin = np.concatenate([np.cos(ang), np.sin(ang)], axis=0)
    csin_bf = np.asarray(csin.astype(ml_dtypes.bfloat16))
    P = np.zeros((ROT, ROT), dtype=np.float64)
    for dp in range(16):
        P[dp + 16, dp] = -1.0
        P[dp, dp + 16] = 1.0
    P_bf = np.asarray(P.astype(ml_dtypes.bfloat16))
    tri = np.where(np.arange(128)[:, None] > np.arange(128)[None, :], NEG, 0.0).astype(np.float32)
    identb = np.asarray(np.eye(128).astype(ml_dtypes.bfloat16))
    identf = np.eye(128, dtype=np.float32)
    ones_p = np.asarray(np.ones((128, 1)).astype(ml_dtypes.bfloat16))
    ones_rf = np.ones((1, 128), dtype=np.float32)
    ones_rb = np.asarray(np.ones((1, 128)).astype(ml_dtypes.bfloat16))
    return csin_bf, P_bf, tri, identb, identf, ones_p, ones_rf, ones_rb


class Ctr:
    def __init__(self, nc, name):
        self.h = nc.alloc_semaphore(name)
        self.n = 0

    def inc(self, instr, k=1):
        instr.then_inc(self.h, k)
        self.n += k
        return self.n


_WSEEN = {}


def W(eng, key, ctr, at=None):
    at = ctr.n if at is None else at
    assert at <= ctr.n, "wait on future count (emission-order bug)"
    if at <= 0:
        return
    k = (key, id(ctr))
    if _WSEEN.get(k, -1) >= at:
        return
    _WSEEN[k] = at
    eng.wait_ge(ctr.h, at)


def build():
    _WSEEN.clear()
    nc = bass.Bass("TRN2", target_bir_lowering=False)
    _uid = [0]

    def nm(b):
        _uid[0] += 1
        return f"{b}_{_uid[0]}"

    xT = nc.dram_tensor("xT", [D, T], F32, kind="ExternalInput").ap()
    wqkvT = nc.dram_tensor("wqkvT", [D, 3 * D], F32, kind="ExternalInput").ap()
    woutT = nc.dram_tensor("woutT", [D, D], F32, kind="ExternalInput").ap()
    gsgT = nc.dram_tensor("gsgT", [D, 32], F32, kind="ExternalInput").ap()
    sguT = nc.dram_tensor("sguT", [D, 2 * DIS], F32, kind="ExternalInput").ap()
    sdownT = nc.dram_tensor("sdownT", [DIS, D], F32, kind="ExternalInput").ap()
    w13T = nc.dram_tensor("w13T", [NEXP, D, 2 * DI], F32, kind="ExternalInput").ap()
    w2T = nc.dram_tensor("w2T", [NEXP, DI, D], F32, kind="ExternalInput").ap()
    y_out = nc.dram_tensor("y_out", [T, D], F32, kind="ExternalOutput").ap()

    wqkvT_bf = nc.dram_tensor("wqkvT_bf", [D, 3 * D], BF16, kind="Internal").ap()
    woutT_bf = nc.dram_tensor("woutT_bf", [D, D], BF16, kind="Internal").ap()
    sguT_bf = nc.dram_tensor("sguT_bf", [D, 2 * DIS], BF16, kind="Internal").ap()
    sdownT_bf = nc.dram_tensor("sdownT_bf", [DIS, D], BF16, kind="Internal").ap()
    w13T_bf = nc.dram_tensor("w13T_bf", [NEXP, D, 2 * DI], BF16, kind="Internal").ap()
    w2T_bf = nc.dram_tensor("w2T_bf", [NEXP, DI, D], BF16, kind="Internal").ap()

    csin_np, P_np, tri_np, identb_np, identf_np, onesp_np, onesrf_np, onesrb_np = _host_consts()
    csin_d = nc.inline_tensor(csin_np, "csin_d").ap()
    prot_d = nc.inline_tensor(P_np, "prot_d").ap()
    tri_d = nc.inline_tensor(tri_np, "tri_d").ap()
    identb_d = nc.inline_tensor(identb_np, "identb_d").ap()
    identf_d = nc.inline_tensor(identf_np, "identf_d").ap()
    onesp_d = nc.inline_tensor(onesp_np, "onesp_d").ap()
    onesrf_d = nc.inline_tensor(onesrf_np, "onesrf_d").ap()
    onesrb_d = nc.inline_tensor(onesrb_np, "onesrb_d").ap()

    taps = {}
    tapshape = {"qkv0": [3 * 128, T], "oh1": [128, NH * HW], "y1": [128, CC * T],
                "coef0": [32, HW], "acc0": [128, CC * HW], "accA": [128, CC * HW],
                "wsb": [128, CC * 384], "qev": [128, HW], "qro": [128, HW], "qed": [128, HW], "ked": [128, T]}
    for t_ in TAPS:
        taps[t_] = nc.dram_tensor(f"tap_{t_}", tapshape[t_], BF16, kind="ExternalOutput").ap()

    xb = nc.alloc_sbuf_tensor("xb", [128, CC * T], BF16)
    oh = nc.alloc_sbuf_tensor("oh", [128, NH * HW], BF16)
    gsg = nc.alloc_sbuf_tensor("gsg", [128, CC * 32], BF16)
    identb = nc.alloc_sbuf_tensor("identb", [128, 128], BF16)
    identf = nc.alloc_sbuf_tensor("identf", [128, 128], F32)
    ones_p = nc.alloc_sbuf_tensor("ones_p", [128, 1], BF16)
    ones_rf = nc.alloc_sbuf_tensor("ones_rf", [1, 128], F32)
    ones_rb = nc.alloc_sbuf_tensor("ones_rb", [1, 128], BF16)
    rowA = nc.alloc_sbuf_tensor("rowA", [1, HW], F32)
    rowB = nc.alloc_sbuf_tensor("rowB", [1, HW], F32)
    mcol = nc.alloc_sbuf_tensor("mcol", [128, 1], F32)
    sscol = nc.alloc_sbuf_tensor("sscol", [128, 4], F32)
    tmpcol = nc.alloc_sbuf_tensor("tmpcol", [128, 1], F32)
    rstdc = nc.alloc_sbuf_tensor("rstdc", [128, 1], F32)
    bcol = nc.alloc_sbuf_tensor("bcol", [128, 1], F32)
    cexpb = nc.alloc_sbuf_tensor("cexpb", [128, 1], F32)
    ceps = nc.alloc_sbuf_tensor("ceps", [128, 1], F32)

    PEc = Ctr(nc, "PEc")
    VEc = Ctr(nc, "VEc")
    SEc = Ctr(nc, "SEc")
    SYc = Ctr(nc, "SYc")
    GPc = Ctr(nc, "GPc")

    def vi(instr):
        return VEc.inc(instr)

    def si(instr):
        return SEc.inc(instr)

    cast_cnt = {}
    const_cnt = {}
    cast_gate_pe = [30]
    cast_queue = []
    for i in range(16):
        cast_queue.append((sguT_bf[i * 128:(i + 1) * 128, :], sguT[i * 128:(i + 1) * 128, :],
                           "sgu" if i == 15 else None))
    for i in range(8):
        cast_queue.append((sdownT_bf[i * 704:(i + 1) * 704, :], sdownT[i * 704:(i + 1) * 704, :],
                           "sdown" if i == 7 else None))
    for e in range(NEXP):
        for i in range(4):
            cast_queue.append((w13T_bf[e][i * 512:(i + 1) * 512, :], w13T[e][i * 512:(i + 1) * 512, :],
                               f"w13_{e}" if i == 3 else None))
        for i in range(2):
            cast_queue.append((w2T_bf[e][i * 704:(i + 1) * 704, :], w2T[e][i * 704:(i + 1) * 704, :],
                               f"w2_{e}" if i == 1 else None))

    with nc.Block() as blk:
        def g_casts(g):
            for i in range(2):
                g.dma_start(wqkvT_bf[i * 1024:(i + 1) * 1024, :],
                            wqkvT[i * 1024:(i + 1) * 1024, :]).then_inc(GPc.h, 16)
                GPc.n += 16
            cast_cnt["wqkv"] = GPc.n
            g.dma_start(xb[:, :].rearrange("p (cc t) -> p cc t", t=T),
                        xT.rearrange("(cc p) t -> p cc t", p=128)).then_inc(GPc.h, 16)
            GPc.n += 16
            cast_cnt["xb"] = GPc.n
            g.wait_ge(PEc.h, cast_gate_pe[0])
            for i in range(2):
                g.dma_start(woutT_bf[i * 1024:(i + 1) * 1024, :],
                            woutT[i * 1024:(i + 1) * 1024, :]).then_inc(GPc.h, 16)
                GPc.n += 16
            cast_cnt["wout"] = GPc.n
            g.dma_start(gsg[:, :].rearrange("p (cc k) -> p cc k", k=32),
                        gsgT.rearrange("(cc p) k -> p cc k", k=32, p=128)).then_inc(GPc.h, 16)
            GPc.n += 16
            cast_cnt["gsg"] = GPc.n
            pass

        blk.gpsimd(g_casts)

        def cast_pump(n):
            if not cast_queue:
                return
            items = [cast_queue.pop(0) for _ in range(min(n, len(cast_queue)))]
            gate = PEc.n

            def f(g):
                if gate > 0:
                    g.wait_ge(PEc.h, gate)
                for dst, srcx, tag in items:
                    g.dma_start(dst, srcx).then_inc(GPc.h, 16)
                    GPc.n += 16
                    if tag is not None:
                        cast_cnt[tag] = GPc.n

            blk.gpsimd(f)

        if STOP_AFTER is not None:
            cast_pump(10 ** 6)

        def s_consts(s):
            for name, dst, src in (("identb", identb, identb_d), ("identf", identf, identf_d),
                                   ("ones_p", ones_p, onesp_d), ("ones_rf", ones_rf, onesrf_d),
                                   ("ones_rb", ones_rb, onesrb_d)):
                s.dma_start(dst[:, :], src[:, :]).then_inc(SYc.h, 16)
                SYc.n += 16
                const_cnt[name] = SYc.n

        blk.sync(s_consts)

        def v_consts(v):
            vi(v.memset(cexpb[:, :], EXPB))
            vi(v.memset(ceps[:, :], 1e-5))

        blk.vector(v_consts)
        const_cnt["cols"] = VEc.n

        es_A = ExitStack()
        csin = es_A.enter_context(nc.sbuf_tensor(nm("csin"), [64, T], BF16))
        prot = es_A.enter_context(nc.sbuf_tensor(nm("prot"), [32, 32], BF16))
        tri = es_A.enter_context(nc.sbuf_tensor(nm("tri"), [128, 128], F32))
        wqkv_sb = es_A.enter_context(nc.sbuf_tensor(nm("wqkv_sb"), [128, 2 * 3 * CC * 256], BF16))

        def s_consts2(s):
            for name, dst, src in (("csin", csin, csin_d), ("prot", prot, prot_d), ("tri", tri, tri_d)):
                s.dma_start(dst[:, :], src[:, :]).then_inc(SYc.h, 16)
                SYc.n += 16
                const_cnt[name] = SYc.n

        blk.sync(s_consts2)

        wqkv_rel = [0, 0]
        wqkv_rdy = {}
        c_oh = {}
        c_y1 = [0] * CC

        def prefetch_pair(half, hp):
            buf = hp % 2

            def f(q):
                W(q, "sy", GPc, cast_cnt["wqkv"])
                W(q, "sy", PEc, wqkv_rel[buf])
                for role in range(3):
                    dstb = wqkv_sb[:, buf * 3 * CC * 256 + role * CC * 256:
                                   buf * 3 * CC * 256 + (role + 1) * CC * 256] \
                        .rearrange("p (cc o) -> p cc o", o=256)
                    src = wqkvT_bf[:, role * D + hp * 256: role * D + (hp + 1) * 256]
                    q.dma_start(dstb, src.rearrange("(cc p) o -> p cc o", p=128)).then_inc(SYc.h, 16)
                    SYc.n += 16
                wqkv_rdy[(half, hp)] = SYc.n

            blk.sync(f)

        def qkv_matmul(psum, role, width, t0, h):
            buf = (h // 2) % 2
            sub = h % 2

            def f(t):
                W(t, "pe", GPc, cast_cnt["w2_15"] if DEV_WAIT_CASTS else cast_cnt["xb"])
                W(t, "pe", VEc)
                W(t, "pe", SEc)
                last = None
                for cc in range(CC):
                    base = buf * 3 * CC * 256 + role * CC * 256 + cc * 256 + sub * 128
                    lhs = wqkv_sb[:, base: base + 128]
                    for s0 in range(0, width, 512):
                        last = t.matmul(psum[:, s0:s0 + 512], lhs,
                                        xb[:, cc * T + t0 + s0: cc * T + t0 + s0 + 512],
                                        start=(cc == 0), stop=(cc == CC - 1))
                PEc.inc(last)

            return f

        for half in (1, 0):
            t0q = half * HW
            jbmax = (t0q + HW) // 128

            es_hscr = ExitStack()
            q_sb = es_hscr.enter_context(nc.sbuf_tensor(nm("q_sb"), [128, HW], BF16))
            k_sb = es_hscr.enter_context(nc.sbuf_tensor(nm("k_sb"), [128, T], BF16))
            vfm = es_hscr.enter_context(nc.sbuf_tensor(nm("vfm"), [128, T], BF16))
            vtm = es_hscr.enter_context(nc.sbuf_tensor(nm("vtm"), [128, T], BF16))
            Ebuf = es_hscr.enter_context(nc.sbuf_tensor(nm("Ebuf"), [128, 2 * HW], BF16))
            stscr = es_hscr.enter_context(nc.sbuf_tensor(nm("stscr"), [128, 128], F32))
            rb_sb = es_hscr.enter_context(nc.sbuf_tensor(nm("rb_sb"), [128, HW], BF16))
            rt1 = es_hscr.enter_context(nc.sbuf_tensor(nm("rt1"), [32, 512], BF16))
            rt2 = es_hscr.enter_context(nc.sbuf_tensor(nm("rt2"), [32, 512], BF16))

            def rotary(psum_rot, dst_sb, width, t0):
                for ch0 in range(0, width, 512):
                    w = min(512, width - ch0)

                    def tmm(t, ch0=ch0, w=w):
                        W(t, "pe", SYc, const_cnt["prot"])
                        W(t, "pe", SEc)
                        W(t, "pe", VEc)
                        PEc.inc(t.matmul(psum_rot[0:32, 0:w], prot[0:32, 0:32],
                                         dst_sb[0:32, ch0:ch0 + w], start=True, stop=True))

                    blk.tensor(tmm)
                    c_rot = PEc.n

                    def vcomb(v, ch0=ch0, w=w, c_rot=c_rot):
                        W(v, "ve", SYc, const_cnt["csin"])
                        W(v, "ve", PEc, c_rot)
                        vi(v.tensor_mul(rt1[0:32, 0:w], dst_sb[0:32, ch0:ch0 + w],
                                        csin[0:32, t0 + ch0:t0 + ch0 + w]))
                        b = vi(v.tensor_mul(rt2[0:32, 0:w], psum_rot[0:32, 0:w],
                                            csin[32:64, t0 + ch0:t0 + ch0 + w]))
                        W(v, "ve", VEc, b)
                        vi(v.tensor_add(dst_sb[0:32, ch0:ch0 + w], rt1[0:32, 0:w], rt2[0:32, 0:w]))

                    blk.vector(vcomb)

            g_se = [SEc.n]
            g_ve_q = [VEc.n]
            g_ve_kv = [VEc.n]
            c_rb = [0]
            prefetch_pair(half, 0)
            for h in range(NH):
                cast_pump(2)
                if h % 2 == 0 and h // 2 + 1 < NH // 2:
                    prefetch_pair(half, h // 2 + 1)

                es_head = ExitStack()
                pav = es_head.enter_context(nc.psum_tensor(nm("pav"), [128, HW], F32))

                kw = jbmax * 128
                ngrp = kw // 512
                es_qkv = ExitStack()
                pq = es_qkv.enter_context(nc.psum_tensor(nm("pq"), [128, HW], F32))
                prr = es_qkv.enter_context(nc.psum_tensor(nm("prr"), [128, 512], F32))
                pkv = es_qkv.enter_context(nc.psum_tensor(nm("pkv"), [128, 1024], F32))
                ptb = es_qkv.enter_context(nc.psum_tensor(nm("ptb"), [128, 512], BF16))

                def wait_w(t, half=half, h=h):
                    W(t, "pe", SYc, wqkv_rdy[(half, h // 2)])

                blk.tensor(wait_w)

                def qmm(t, h=h, t0q=t0q, gse=g_se[0], gveq=g_ve_q[0]):
                    W(t, "pe", GPc, cast_cnt["xb"])
                    W(t, "pe", SEc, gse)
                    W(t, "pe", VEc, gveq)
                    buf = (h // 2) % 2
                    sub = h % 2
                    last = None
                    for cc in range(CC):
                        base = buf * 3 * CC * 256 + 0 * CC * 256 + cc * 256 + sub * 128
                        lhs = wqkv_sb[:, base: base + 128]
                        for s0 in range(0, HW, 512):
                            last = t.matmul(pq[:, s0:s0 + 512], lhs,
                                            xb[:, cc * T + t0q + s0: cc * T + t0q + s0 + 512],
                                            start=(cc == 0), stop=(cc == CC - 1))
                    PEc.inc(last)

                blk.tensor(qmm)
                c_pq = PEc.n

                def qevac(sc, c_pq=c_pq):
                    W(sc, "se", PEc, c_pq)
                    si(sc.activation(q_sb[:, :], pq[:, :], AF.Copy, scale=QSCALE))

                blk.scalar(qevac)
                c_qev = SEc.n

                c_kev = {}
                c_vev = {}
                for g4 in range(ngrp):
                    def kmm(t, g4=g4, h=h, gvekv=g_ve_kv[0], gse=g_se[0]):
                        if g4 == 0:
                            W(t, "pe", VEc, gvekv)
                            W(t, "pe", SEc, gse)
                        if g4 >= 2:
                            W(t, "pe", SEc, c_kev[g4 - 2])
                        buf = (h // 2) % 2
                        sub = h % 2
                        last = None
                        for cc in range(CC):
                            base = buf * 3 * CC * 256 + 1 * CC * 256 + cc * 256 + sub * 128
                            lhs = wqkv_sb[:, base: base + 128]
                            last = t.matmul(pkv[:, (g4 % 2) * 512:(g4 % 2 + 1) * 512], lhs,
                                            xb[:, cc * T + g4 * 512: cc * T + (g4 + 1) * 512],
                                            start=(cc == 0), stop=(cc == CC - 1))
                        PEc.inc(last)

                    blk.tensor(kmm)
                    c_k = PEc.n

                    def kev(sc, g4=g4, c_k=c_k):
                        W(sc, "se", PEc, c_k)
                        c_kev[g4] = si(sc.activation(k_sb[:, g4 * 512:(g4 + 1) * 512],
                                                     pkv[:, (g4 % 2) * 512:(g4 % 2 + 1) * 512], AF.Copy))

                    blk.scalar(kev)

                rot_chunks = [(q_sb, 0, t0q, ("q", 0)), (q_sb, 512, t0q + 512, ("q", 0))] + \
                             [(k_sb, g * 512, g * 512, ("k", g)) for g in range(ngrp)]
                c_t2 = {}
                c_radd = {}

                def rot_pmm(idx):
                    dst, ch0, tb, dep = rot_chunks[idx]

                    def f(t, idx=idx, dst=dst, ch0=ch0, dep=dep):
                        W(t, "pe", SYc, const_cnt["prot"])
                        W(t, "pe", SEc, c_qev if dep[0] == "q" else c_kev[dep[1]])
                        if idx >= 1:
                            W(t, "pe", VEc, c_t2.get(idx - 1, 0))
                        PEc.inc(t.matmul(prr[0:32, 0:512], prot[0:32, 0:32],
                                         dst[0:32, ch0:ch0 + 512], start=True, stop=True))

                    blk.tensor(f)
                    c_radd[("pmm", idx)] = PEc.n

                def rot_comb(idx):
                    dst, ch0, tb, dep = rot_chunks[idx]

                    def f(v, idx=idx, dst=dst, ch0=ch0, tb=tb):
                        W(v, "ve", SYc, const_cnt["csin"])
                        W(v, "ve", PEc, c_radd[("pmm", idx)])
                        W(v, "ve", SEc, c_qev if rot_chunks[idx][3][0] == "q" else c_kev[rot_chunks[idx][3][1]])
                        vi(v.tensor_mul(rt1[0:32, :], dst[0:32, ch0:ch0 + 512],
                                        csin[0:32, tb:tb + 512]))
                        b = vi(v.tensor_mul(rt2[0:32, :], prr[0:32, 0:512],
                                            csin[32:64, tb:tb + 512]))
                        c_t2[idx] = b
                        W(v, "ve", VEc, b)
                        c_radd[("add", idx)] = vi(v.tensor_add(dst[0:32, ch0:ch0 + 512],
                                                               rt1[0:32, :], rt2[0:32, :]))

                    blk.vector(f)

                rot_pmm(0)
                rot_comb(0)
                rot_next = [1]

                last_vmm_pe = [0]
                for g4 in range(ngrp):
                    def vmm(t, g4=g4, h=h, ngrp=ngrp):
                        if g4 < 2:
                            W(t, "pe", SEc, c_kev[ngrp - 2 + g4])
                        else:
                            W(t, "pe", SEc, c_vev[g4 - 2])
                        buf = (h // 2) % 2
                        sub = h % 2
                        last = None
                        for cc in range(CC):
                            base = buf * 3 * CC * 256 + 2 * CC * 256 + cc * 256 + sub * 128
                            lhs = wqkv_sb[:, base: base + 128]
                            last = t.matmul(pkv[:, (g4 % 2) * 512:(g4 % 2 + 1) * 512], lhs,
                                            xb[:, cc * T + g4 * 512: cc * T + (g4 + 1) * 512],
                                            start=(cc == 0), stop=(cc == CC - 1))
                        PEc.inc(last)
                        last_vmm_pe[0] = PEc.n

                    blk.tensor(vmm)
                    c_v = PEc.n

                    def vev(sc, g4=g4, c_v=c_v):
                        W(sc, "se", PEc, c_v)
                        c_vev[g4] = si(sc.activation(vfm[:, g4 * 512:(g4 + 1) * 512],
                                                     pkv[:, (g4 % 2) * 512:(g4 % 2 + 1) * 512], AF.Copy))

                    blk.scalar(vev)
                    if rot_next[0] < len(rot_chunks):
                        rot_pmm(rot_next[0])
                        rot_comb(rot_next[0])
                        rot_next[0] += 1

                if h % 2 == 1:
                    wqkv_rel[(h // 2) % 2] = last_vmm_pe[0]

                c_vt_ev = {}
                for grp in range(ngrp):
                    def vtr(t, grp=grp, gvekv=g_ve_kv[0]):
                        W(t, "pe", SYc, const_cnt["identb"])
                        W(t, "pe", SEc, c_vev[grp])
                        W(t, "pe", VEc, c_vt_ev.get(grp - 1, gvekv))
                        last = None
                        for jb in range(grp * 4, grp * 4 + 4):
                            last = t.transpose(ptb[:, (jb % 4) * 128:((jb % 4) + 1) * 128],
                                               vfm[:, jb * 128:(jb + 1) * 128], identb[:, :])
                        PEc.inc(last)

                    blk.tensor(vtr)
                    c_tb = PEc.n

                    def vte(v, grp=grp, c_tb=c_tb):
                        W(v, "ve", PEc, c_tb)
                        c_vt_ev[grp] = vi(v.tensor_copy(vtm[:, grp * 512:(grp + 1) * 512],
                                                        ptb[:, :]))

                    blk.vector(vte)
                    if rot_next[0] < len(rot_chunks):
                        rot_pmm(rot_next[0])
                        rot_comb(rot_next[0])
                        rot_next[0] += 1
                while rot_next[0] < len(rot_chunks):
                    rot_pmm(rot_next[0])
                    rot_comb(rot_next[0])
                    rot_next[0] += 1
                c_vtm = VEc.n
                c_qrot = max(c_radd[("add", i)] for i in range(len(rot_chunks)))
                c_krot = c_qrot
                es_qkv.close()

                es_jb = ExitStack()
                pst = es_jb.enter_context(nc.psum_tensor(nm("pst"), [128, 2 * HW], F32))
                pden = es_jb.enter_context(nc.psum_tensor(nm("pden"), [128, 512], F32))
                pden2 = es_jb.enter_context(nc.psum_tensor(nm("pden2"), [128, 512], F32))
                c_st = {}
                c_exp = {}
                c_av = {}
                jbA_last = t0q // 128 + 3

                def emit_dav(jb, region):
                    il = max(0, jb * 128 - t0q)
                    slot = jb % 2
                    if region == "A" and il >= 512:
                        return

                    def dav(t, jb=jb, il=il, slot=slot, region=region):
                        W(t, "pe", SEc, c_exp[(jb, region)])
                        W(t, "pe", SYc, const_cnt["ones_p"])
                        if jb == 0:
                            W(t, "pe", VEc, c_vtm)
                        st_ = (jb == 0)
                        last = None
                        if region == "A":
                            spA = (jb == min(jbA_last, jbmax - 1))
                            t.matmul(pden[0:1, il:512], ones_p[:, 0:1],
                                     Ebuf[:, slot * HW + il: slot * HW + 512], start=st_, stop=spA)
                            last = t.matmul(pav[:, il:512], vtm[:, jb * 128:(jb + 1) * 128],
                                            Ebuf[:, slot * HW + il: slot * HW + 512], start=st_, stop=spA)
                        else:
                            spB = (jb == jbmax - 1)
                            lo2 = max(il, 512)
                            t.matmul(pden2[0:1, lo2 - 512:512], ones_p[:, 0:1],
                                     Ebuf[:, slot * HW + lo2: slot * HW + HW], start=st_, stop=spB)
                            last = t.matmul(pav[:, lo2:HW], vtm[:, jb * 128:(jb + 1) * 128],
                                            Ebuf[:, slot * HW + lo2: slot * HW + HW], start=st_, stop=spB)
                        c_av[(jb, region)] = PEc.inc(last)

                    blk.tensor(dav)

                for jb in range(jbmax):
                    il = max(0, jb * 128 - t0q)
                    slot = jb % 2
                    diag = (jb * 128 >= t0q)
                    diag_in_A = diag and il < 512

                    def st_mm(t, jb=jb, il=il, slot=slot):
                        W(t, "pe", VEc, max(c_qrot, c_krot))
                        if jb >= 2:
                            W(t, "pe", SEc, c_exp[(jb - 2, "B")])
                        else:
                            W(t, "pe", SEc)
                            W(t, "pe", VEc)
                        c_st[(jb, "A")] = 0
                        for s0 in range(0, HW, 512):
                            if s0 + 512 <= il:
                                continue
                            lo = max(s0, il)
                            mm = t.matmul(pst[:, slot * HW + lo: slot * HW + s0 + 512],
                                          k_sb[:, jb * 128:(jb + 1) * 128],
                                          q_sb[:, lo:s0 + 512], start=True, stop=True)
                            c_st[(jb, "A" if s0 == 0 else "B")] = PEc.inc(mm)

                    blk.tensor(st_mm)

                    if diag:
                        def mask_add(v, jb=jb, il=il, slot=slot):
                            W(v, "ve", PEc, c_st[(jb, "A" if il < 512 else "B")])
                            W(v, "ve", SYc, const_cnt["tri"])
                            vi(v.tensor_add(pst[:, slot * HW + il: slot * HW + il + 128],
                                            pst[:, slot * HW + il: slot * HW + il + 128], tri[:, :]))

                        blk.vector(mask_add)
                        c_mask = VEc.n
                    else:
                        c_mask = 0

                    def expf(sc, jb=jb, il=il, slot=slot, diag_in_A=diag_in_A, diag=diag, c_mask=c_mask):
                        W(sc, "se", VEc, const_cnt["cols"])
                        if jb >= 2:
                            W(sc, "se", PEc, c_av.get((jb - 2, "B"), 0))
                        if il < 512:
                            W(sc, "se", PEc, c_st[(jb, "A")])
                            if diag_in_A:
                                W(sc, "se", VEc, c_mask)
                            c_exp[(jb, "A")] = si(sc.activation(
                                Ebuf[:, slot * HW + il: slot * HW + 512],
                                pst[:, slot * HW + il: slot * HW + 512],
                                AF.Exp, bias=cexpb[:, 0:1]))
                        lo2 = max(il, 512)
                        W(sc, "se", PEc, c_st[(jb, "B")])
                        if diag and not diag_in_A:
                            W(sc, "se", VEc, c_mask)
                        c_exp[(jb, "B")] = si(sc.activation(
                            Ebuf[:, slot * HW + lo2: slot * HW + HW],
                            pst[:, slot * HW + lo2: slot * HW + HW],
                            AF.Exp, bias=cexpb[:, 0:1]))

                    blk.scalar(expf)

                    if jb >= 1:
                        emit_dav(jb - 1, "A")
                        emit_dav(jb - 1, "B")
                    if jb == jbmax - 1:
                        emit_dav(jb, "A")
                        emit_dav(jb, "B")

                if "qed" in TAPS and half == 1 and h == TAP_HEAD:
                    def tqed(g):
                        W(g, "gp", PEc, c_av[(jbmax - 1, "B")])
                        g.dma_start(taps["qed"][:, :], q_sb[:, :]).then_inc(GPc.h, 16)
                        GPc.n += 16
                        g.dma_start(taps["ked"][:, :], k_sb[:, :]).then_inc(GPc.h, 16)
                        GPc.n += 16

                    blk.gpsimd(tqed)
                    c_tqed = GPc.n

                    def tqed2(t, c_tqed=c_tqed):
                        W(t, "pe", GPc, c_tqed)

                    blk.tensor(tqed2)

                def recipd(v):
                    W(v, "ve", PEc, c_av[(jbmax - 1, "B")])
                    vi(v.reciprocal(rowA[0:1, 0:512], pden[0:1, 0:512]))
                    vi(v.reciprocal(rowA[0:1, 512:HW], pden2[0:1, 0:512]))

                blk.vector(recipd)
                c_rr = VEc.n
                es_jb.close()

                es_post = ExitStack()
                prb = es_post.enter_context(nc.psum_tensor(nm("prb"), [128, HW], F32))

                def bcast_r(t, c_rr=c_rr, jbmax=jbmax):
                    W(t, "pe", VEc, c_rr)
                    W(t, "pe", SEc, c_exp[(jbmax - 1, "B")])
                    W(t, "pe", SYc, const_cnt["ones_rf"])
                    last = None
                    for s0 in range(0, HW, 512):
                        last = t.matmul(prb[:, s0:s0 + 512], ones_rf[0:1, :],
                                        rowA[0:1, s0:s0 + 512], start=True, stop=True)
                    PEc.inc(last)

                blk.tensor(bcast_r)
                c_prb = PEc.n

                def ohev(v, h=h, half=half, c_prb=c_prb):
                    W(v, "ve", PEc, c_prb)
                    b = vi(v.tensor_copy(rb_sb[:, :], prb[:, :]))
                    c_rb[0] = b
                    W(v, "ve", VEc, b)
                    c_oh[(half, h)] = vi(v.tensor_mul(oh[:, h * HW:(h + 1) * HW], pav[:, :], rb_sb[:, :]))

                blk.vector(ohev)
                es_post.close()
                es_head.close()
                g_se[0] = c_exp[(jbmax - 1, "B")]
                g_ve_kv[0] = c_rr
                g_ve_q[0] = c_rb[0]

                if "qkv0" in TAPS and half == 1 and h == TAP_HEAD:
                    def tqkv(g):
                        W(g, "gp", VEc, c_oh[(1, TAP_HEAD)])
                        g.dma_start(taps["qkv0"][0:128, 0:HW], q_sb[:, :]).then_inc(GPc.h, 16)
                        GPc.n += 16
                        g.dma_start(taps["qkv0"][128:256, :], k_sb[:, :]).then_inc(GPc.h, 16)
                        GPc.n += 16
                        g.dma_start(taps["qkv0"][256:384, :], vtm[:, :]).then_inc(GPc.h, 16)
                        GPc.n += 16

                    blk.gpsimd(tqkv)
                    c_tapq = GPc.n

                    def twv(v, c_tapq=c_tapq):
                        W(v, "ve", GPc, c_tapq)

                    blk.vector(twv)

                    def tws(sc, c_tapq=c_tapq):
                        W(sc, "se", GPc, c_tapq)

                    blk.scalar(tws)

            if "oh1" in TAPS and half == 1:
                def toh(g):
                    W(g, "gp", VEc, c_oh[(1, NH - 1)])
                    g.dma_start(taps["oh1"][:, :], oh[:, :]).then_inc(GPc.h, 16)
                    GPc.n += 16

                blk.gpsimd(toh)

            es_hscr.close()

            es_w = ExitStack()
            wout_sb = es_w.enter_context(nc.sbuf_tensor(nm("wout_sb"), [128, 2 * CC * 256], BF16))
            sq_sb = es_w.enter_context(nc.sbuf_tensor(nm("sq_sb"), [128, 2 * HW], BF16))
            Abc = es_w.enter_context(nc.sbuf_tensor(nm("Abc"), [128, HW], F32))
            Bbc = es_w.enter_context(nc.sbuf_tensor(nm("Bbc"), [128, HW], F32))
            t1scr = es_w.enter_context(nc.sbuf_tensor(nm("t1scr"), [128, HW], F32))

            es_wp = ExitStack()
            pw = [es_wp.enter_context(nc.psum_tensor(nm(f"pw{i}"), [128, HW], F32)) for i in range(2)]
            psm = es_wp.enter_context(nc.psum_tensor(nm("psm"), [128, 512], F32))
            psm2 = es_wp.enter_context(nc.psum_tensor(nm("psm2"), [128, 512], F32))
            pss = es_wp.enter_context(nc.psum_tensor(nm("pss"), [128, 512], F32))
            pss2 = es_wp.enter_context(nc.psum_tensor(nm("pss2"), [128, 512], F32))

            wout_rel = [PEc.n, PEc.n]
            wout_gate_ve = VEc.n
            wout_rdy = {}
            c_p1 = {}
            c_sq = {}
            c_stat = {}
            for cc in range(CC):
                wbuf = (cc // 2) % 2

                if cc % 2 == 0:
                    def wload(q, cc=cc, wbuf=wbuf, wout_gate_ve=wout_gate_ve):
                        W(q, "sy", GPc, cast_cnt["wout"])
                        W(q, "sy", PEc, wout_rel[wbuf])
                        W(q, "sy", VEc, wout_gate_ve)
                        dstb = wout_sb[:, wbuf * CC * 256:(wbuf + 1) * CC * 256] \
                            .rearrange("p (fc o) -> p fc o", o=256)
                        q.dma_start(dstb, woutT_bf[:, cc * 128:(cc + 2) * 128]
                                    .rearrange("(fc p) o -> p fc o", p=128)).then_inc(SYc.h, 16)
                        SYc.n += 16
                        wout_rdy[cc // 2] = SYc.n

                    blk.sync(wload)

                def wmm(t, cc=cc, wbuf=wbuf, half=half):
                    W(t, "pe", SYc, wout_rdy[cc // 2])
                    W(t, "pe", VEc, c_oh[(half, NH - 1)])
                    if cc < 2:
                        W(t, "pe", VEc)
                        W(t, "pe", SEc)
                    if cc >= 2:
                        W(t, "pe", VEc, c_p1[cc - 2])
                    sub = cc % 2
                    last = None
                    for fc in range(CC):
                        lhs = wout_sb[:, wbuf * CC * 256 + fc * 256 + sub * 128:
                                      wbuf * CC * 256 + fc * 256 + sub * 128 + 128]
                        for s0 in range(0, HW, 512):
                            last = t.matmul(pw[cc % 2][:, s0:s0 + 512], lhs,
                                            oh[:, fc * HW + s0: fc * HW + s0 + 512],
                                            start=(fc == 0), stop=(fc == CC - 1))
                    PEc.inc(last)
                    if cc % 2 == 1:
                        wout_rel[wbuf] = PEc.n

                blk.tensor(wmm)
                c_pw = PEc.n

                def p1f(v, cc=cc, c_pw=c_pw, t0q=t0q):
                    W(v, "ve", PEc, c_pw)
                    c_p1[cc] = vi(v.tensor_add(xb[:, cc * T + t0q: cc * T + t0q + HW],
                                               pw[cc % 2][:, :], xb[:, cc * T + t0q: cc * T + t0q + HW]))

                blk.vector(p1f)

                def sqf(sc, cc=cc, t0q=t0q):
                    W(sc, "se", VEc, c_p1[cc])
                    if cc >= 2:
                        W(sc, "se", PEc, c_stat.get(cc - 2, 0))
                    si(sc.activation(sq_sb[:, (cc % 2) * HW:(cc % 2 + 1) * HW],
                                     xb[:, cc * T + t0q: cc * T + t0q + HW], AF.Square))
                    c_sq[cc] = SEc.n

                blk.scalar(sqf)

                def statf_emit(cc):
                    def statf(t, cc=cc, t0q=t0q):
                        W(t, "pe", VEc, c_p1[cc])
                        W(t, "pe", SEc, c_sq[cc])
                        sl = (cc % 2) * HW
                        st_, sp_ = (cc == 0), (cc == CC - 1)
                        t.matmul(psm[0:1, :], ones_p[:, 0:1],
                                 xb[:, cc * T + t0q: cc * T + t0q + 512], start=st_, stop=sp_)
                        t.matmul(psm2[0:1, :], ones_p[:, 0:1],
                                 xb[:, cc * T + t0q + 512: cc * T + t0q + HW], start=st_, stop=sp_)
                        t.matmul(pss[0:1, :], ones_p[:, 0:1], sq_sb[:, sl:sl + 512], start=st_, stop=sp_)
                        last = t.matmul(pss2[0:1, :], ones_p[:, 0:1], sq_sb[:, sl + 512:sl + HW],
                                        start=st_, stop=sp_)
                        c_stat[cc] = PEc.inc(last)

                    blk.tensor(statf)

                if cc >= 1:
                    statf_emit(cc - 1)
                if cc == CC - 1:
                    statf_emit(cc)

            def rows_math(v):
                W(v, "ve", PEc, c_stat[CC - 1])
                vi(v.tensor_scalar_mul(rowA[0:1, 0:512], psm[0:1, :], 1.0 / D))
                vi(v.tensor_scalar_mul(rowA[0:1, 512:HW], psm2[0:1, :], 1.0 / D))
                vi(v.tensor_scalar_mul(rowB[0:1, 0:512], pss[0:1, :], 1.0 / D))
                b = vi(v.tensor_scalar_mul(rowB[0:1, 512:HW], pss2[0:1, :], 1.0 / D))
                W(v, "ve", VEc, b)
                vi(v.tensor_mul(psm[0:1, 0:512], rowA[0:1, 0:512], rowA[0:1, 0:512]))
                c = vi(v.tensor_mul(psm2[0:1, 0:512], rowA[0:1, 512:HW], rowA[0:1, 512:HW]))
                W(v, "ve", VEc, c)
                vi(v.tensor_sub(rowB[0:1, 0:512], rowB[0:1, 0:512], psm[0:1, 0:512]))
                vi(v.tensor_sub(rowB[0:1, 512:HW], rowB[0:1, 512:HW], psm2[0:1, 0:512]))

            blk.vector(rows_math)
            c_var = VEc.n

            def rows_sqrt(sc):
                W(sc, "se", VEc, c_var)
                W(sc, "se", VEc, const_cnt["cols"])
                si(sc.activation(rowB[0:1, 0:HW], rowB[0:1, 0:HW], AF.Sqrt, bias=ceps[0:1, 0:1]))

            blk.scalar(rows_sqrt)
            c_sqr = SEc.n

            def rows_fin(v):
                W(v, "ve", SEc, c_sqr)
                a = vi(v.reciprocal(rowB[0:1, 0:HW], rowB[0:1, 0:HW]))
                W(v, "ve", VEc, a)
                vi(v.tensor_mul(rowA[0:1, 0:HW], rowA[0:1, 0:HW], rowB[0:1, 0:HW]))

            blk.vector(rows_fin)
            c_rows = VEc.n

            def bcast_ln(t):
                W(t, "pe", VEc, c_rows)
                last = None
                for s0 in range(0, HW, 512):
                    t.matmul(pw[0][:, s0:s0 + 512], ones_rf[0:1, :], rowB[0:1, s0:s0 + 512],
                             start=True, stop=True)
                    last = t.matmul(pw[1][:, s0:s0 + 512], ones_rf[0:1, :], rowA[0:1, s0:s0 + 512],
                                    start=True, stop=True)
                PEc.inc(last)

            blk.tensor(bcast_ln)
            c_bc = PEc.n

            def bc_ev(v, c_bc=c_bc):
                W(v, "ve", PEc, c_bc)
                vi(v.tensor_copy(Abc[:, :], pw[0][:, :]))
                vi(v.tensor_copy(Bbc[:, :], pw[1][:, :]))

            blk.vector(bc_ev)
            c_ab = VEc.n

            for cc in range(CC):
                def y1f(v, cc=cc, t0q=t0q):
                    W(v, "ve", VEc, c_ab)
                    a = vi(v.tensor_mul(t1scr[:, :], xb[:, cc * T + t0q: cc * T + t0q + HW], Abc[:, :]))
                    W(v, "ve", VEc, a)
                    c_y1[cc] = max(c_y1[cc], vi(v.tensor_sub(xb[:, cc * T + t0q: cc * T + t0q + HW],
                                                             t1scr[:, :], Bbc[:, :])))

                blk.vector(y1f)

            es_wp.close()
            es_w.close()

        es_A.close()
        c_y1_all = max(c_y1)

        if "y1" in TAPS:
            def ty1(g):
                W(g, "gp", VEc, c_y1_all)
                g.dma_start(taps["y1"][:, :], xb[:, :]).then_inc(GPc.h, 16)
                GPc.n += 16

            blk.gpsimd(ty1)
        if STOP_AFTER in ("attn", "y1"):
            def fin0(g):
                W(g, "gp", GPc)

            blk.gpsimd(fin0)
            return nc

        es_F = ExitStack()
        w13_sb = es_F.enter_context(nc.sbuf_tensor(nm("w13_sb"), [128, 4 * CC * 256], BF16))
        w2_sb = es_F.enter_context(nc.sbuf_tensor(nm("w2_sb"), [128, 2 * NIC * 256], BF16))
        h_sb = es_F.enter_context(nc.sbuf_tensor(nm("h_sb"), [128, NIC * HW], BF16))
        coef_fm = es_F.enter_context(nc.sbuf_tensor(nm("coef_fm"), [32, HW], BF16))
        crow = es_F.enter_context(nc.sbuf_tensor(nm("crow"), [1, HW], BF16))

        specs = []
        for g in range(4):
            specs.append(("sh", sguT_bf, g * 2 * DI, 0,
                          sdownT_bf[g * DI:(g + 1) * DI, :], 16, "sgu", "sdown"))
        for e in range(NEXP):
            specs.append(("ex", w13T_bf[e], 0, 0, w2T_bf[e], e,
                          f"w13_{e}", f"w2_{e}"))

        ffn_gate_pe = PEc.n
        ffn_gate_ve = VEc.n
        w13_rel = [ffn_gate_pe] * 4
        w13_rdy = {}
        w2_rel = [ffn_gate_pe] * 2
        w2_rdy = {}

        for th in range(2):
            t0 = th * HW

            es_R = ExitStack()
            lg_sb = es_R.enter_context(nc.sbuf_tensor(nm("lg_sb"), [17, HW], F32))
            rtm = es_R.enter_context(nc.sbuf_tensor(nm("rtm"), [128, 8 * 32], F32))
            ctm = es_R.enter_context(nc.sbuf_tensor(nm("ctm"), [128, 8 * 32], F32))
            r8 = es_R.enter_context(nc.sbuf_tensor(nm("r8"), [128, 8], F32))
            rcol = es_R.enter_context(nc.sbuf_tensor(nm("rcol"), [128, 1], F32))
            zapt = es_R.enter_context(nc.sbuf_tensor(nm("zapt"), [128, 16], F32))

            es_Rp = ExitStack()
            plg = es_Rp.enter_context(nc.psum_tensor(nm("plg"), [128, HW], F32))
            ptt = es_Rp.enter_context(nc.psum_tensor(nm("ptt"), [128, 512], F32))

            def rlog(t, t0=t0):
                W(t, "pe", GPc, cast_cnt["gsg"])
                W(t, "pe", VEc, c_y1_all)
                W(t, "pe", VEc)
                W(t, "pe", SEc)
                last = None
                for cc in range(CC):
                    lhs = gsg[:, cc * 32: cc * 32 + 17]
                    for s0 in range(0, HW, 512):
                        last = t.matmul(plg[0:17, s0:s0 + 512], lhs,
                                        xb[:, cc * T + t0 + s0: cc * T + t0 + s0 + 512],
                                        start=(cc == 0), stop=(cc == CC - 1))
                PEc.inc(last)

            blk.tensor(rlog)
            c_lg = PEc.n

            def rlev(sc, c_lg=c_lg):
                W(sc, "se", PEc, c_lg)
                si(sc.activation(lg_sb[:, :], plg[0:17, :], AF.Copy))

            blk.scalar(rlev)
            c_lge = SEc.n

            c_rtm = {}
            for grp in range(2):
                def rtr(t, grp=grp, c_lge=c_lge):
                    W(t, "pe", SEc, c_lge)
                    W(t, "pe", SYc, const_cnt["identf"])
                    if grp == 1:
                        W(t, "pe", VEc, c_rtm[0])
                    last = None
                    for k in range(4):
                        last = t.transpose(ptt[:, k * 128:k * 128 + 17],
                                           lg_sb[:, (grp * 4 + k) * 128:(grp * 4 + k + 1) * 128],
                                           identf[0:17, 0:17])
                    PEc.inc(last)

                blk.tensor(rtr)
                c_t = PEc.n

                def rte(v, grp=grp, c_t=c_t):
                    W(v, "ve", PEc, c_t)
                    src = ptt[:, :].rearrange("p (k o) -> p k o", o=128)[:, :, 0:32]
                    dst = rtm[:, grp * 128:(grp + 1) * 128].rearrange("p (k o) -> p k o", o=32)
                    c_rtm[grp] = vi(v.tensor_copy(dst, src))

                blk.vector(rte)

            c_ctm = {}
            for ch in range(8):
                base = ch * 32

                def soft(v, ch=ch, base=base):
                    W(v, "ve", VEc, c_rtm[ch // 4])
                    c_ctm[("max", ch)] = vi(v.tensor_reduce(rcol[:, 0:1], rtm[:, base:base + 16],
                                                            AX.X, ALU.max, negate=True))

                blk.vector(soft)

                def sexp(sc, ch=ch, base=base):
                    W(sc, "se", VEc, c_ctm[("max", ch)])
                    si(sc.activation(ctm[:, base:base + 16], rtm[:, base:base + 16], AF.Exp,
                                     bias=rcol[:, 0:1]))
                    c_ctm[("exp", ch)] = SEc.n

                blk.scalar(sexp)

                def top4(v, ch=ch, base=base):
                    W(v, "ve", SEc, c_ctm[("exp", ch)])
                    a = vi(v.reduce_sum(rcol[:, 0:1], ctm[:, base:base + 16], AX.X))
                    W(v, "ve", VEc, a)
                    a = vi(v.reciprocal(rcol[:, 0:1], rcol[:, 0:1]))
                    b = vi(v.max(out=r8[:, :], in_=ctm[:, base:base + 16]))
                    W(v, "ve", VEc, b)
                    b = vi(v.memset(r8[:, 4:8], -1.0))
                    W(v, "ve", VEc, b)
                    b = vi(v.match_replace(out=zapt[:, :], in_to_replace=r8[:, :],
                                           in_values=ctm[:, base:base + 16], imm_value=0.0))
                    W(v, "ve", VEc, b)
                    b = vi(v.tensor_sub(zapt[:, :], ctm[:, base:base + 16], zapt[:, :]))
                    W(v, "ve", VEc, b)
                    W(v, "ve", VEc, a)
                    c_ctm[("coef", ch)] = vi(v.tensor_single_scalar(ctm[:, base:base + 16],
                                                                    zapt[:, :], rcol[:, 0:1], ALU.mult))

                blk.vector(top4)

                def sigg(sc, ch=ch, base=base):
                    W(sc, "se", VEc, c_ctm[("coef", ch)])
                    c_ctm[("sig", ch)] = si(sc.activation(ctm[:, base + 16:base + 17],
                                                          rtm[:, base + 16:base + 17], AF.Sigmoid))

                blk.scalar(sigg)

            c_cf = {}
            for grp in range(2):
                def ctr_(t, grp=grp):
                    for ch in range(grp * 4, grp * 4 + 4):
                        W(t, "pe", SEc, c_ctm[("sig", ch)])
                        W(t, "pe", VEc, c_ctm[("coef", ch)])
                    if grp == 1:
                        W(t, "pe", VEc, c_cf[0])
                    last = None
                    for k in range(4):
                        ch = grp * 4 + k
                        last = t.transpose(ptt[0:32, k * 128:(k + 1) * 128],
                                           ctm[:, ch * 32:(ch + 1) * 32], identf[:, :])
                    PEc.inc(last)

                blk.tensor(ctr_)
                c_t2 = PEc.n

                def cfe(v, grp=grp, c_t2=c_t2):
                    W(v, "ve", PEc, c_t2)
                    src = ptt[0:32, :].rearrange("p (k o) -> p k o", o=128)
                    dst = coef_fm[0:32, grp * 512:(grp + 1) * 512].rearrange("p (k o) -> p k o", o=128)
                    c_cf[grp] = vi(v.tensor_copy(dst, src))

                blk.vector(cfe)

            c_coef = VEc.n
            es_Rp.close()
            es_R.close()

            if "coef0" in TAPS and th == 0:
                def tcoef(g):
                    W(g, "gp", VEc, c_coef)
                    g.dma_start(taps["coef0"][:, :], coef_fm[:, :]).then_inc(GPc.h, 16)
                    GPc.n += 16

                blk.gpsimd(tcoef)
            if STOP_AFTER == "router" and th == 0:
                def fin1(g):
                    W(g, "gp", GPc)

                blk.gpsimd(fin1)
                return nc

            es_E = ExitStack()
            silg = es_E.enter_context(nc.sbuf_tensor(nm("silg"), [128, HW], BF16))
            h1s = es_E.enter_context(nc.sbuf_tensor(nm("h1s"), [128, HW], BF16))
            bca = es_E.enter_context(nc.sbuf_tensor(nm("bca"), [128, HW], BF16))

            c_bca = [0]
            c_crow_pe = [0]
            c_pd_last = [0]
            pd_rel = [0, 0]
            c_accL = {}

            c_cr_last = [0]

            def emit_crow(spec_idx):
                ridx = specs[spec_idx][5]

                def gcrow(g):
                    W(g, "gp", VEc, c_coef)
                    W(g, "gp", PEc, c_crow_pe[0])
                    g.dma_start(crow[0:1, :], coef_fm[ridx:ridx + 1, :]).then_inc(GPc.h, 16)
                    GPc.n += 16

                blk.gpsimd(gcrow)
                c_cr_last[0] = GPc.n

            def emit_bcast_mm(spec_idx, pbb):
                c_cr = c_cr_last[0]

                def tbb(t, c_cr=c_cr):
                    W(t, "pe", GPc, c_cr)
                    W(t, "pe", VEc)
                    W(t, "pe", SEc)
                    W(t, "pe", SYc, const_cnt["ones_rb"])
                    last = None
                    for s0 in range(0, HW, 512):
                        last = t.matmul(pbb[:, s0:s0 + 512], ones_rb[0:1, :],
                                        crow[0:1, s0:s0 + 512], start=True, stop=True)
                    PEc.inc(last)
                    c_crow_pe[0] = PEc.n

                blk.tensor(tbb)
                c_pb = PEc.n

                def vbb(v, c_pb=c_pb):
                    W(v, "ve", PEc, c_pb)
                    c_bca[0] = vi(v.tensor_copy(bca[:, :], pbb[:, :]))

                blk.vector(vbb)

            es_bc0 = ExitStack()
            pbb0 = es_bc0.enter_context(nc.psum_tensor(nm("pbb0"), [128, HW], F32))
            emit_crow(0)
            emit_bcast_mm(0, pbb0)
            es_bc0.close()

            specs_run = specs if DEV_N_EXPERTS is None else specs[:DEV_N_EXPERTS]
            for si_, spec in enumerate(specs_run):
                kind, upsrc, gcol, ucol, dnsrc, ridx, tag_up, tag_dn = spec
                cast_pump(4)
                up3 = upsrc.rearrange("(cc p) o -> p cc o", p=128)
                dn3 = dnsrc.rearrange("(ic p) c -> p ic c", p=128)
                my = {}

                if si_ + 1 < len(specs_run):
                    emit_crow(si_ + 1)
                es_up = ExitStack()
                pg = [es_up.enter_context(nc.psum_tensor(nm(f"pg{i}"), [128, HW], F32)) for i in range(2)]
                pu = [es_up.enter_context(nc.psum_tensor(nm(f"pu{i}"), [128, HW], F32)) for i in range(2)]
                for ic in range(NIC):
                    slot = ic % 4

                    def wl(q, slot=slot, ic=ic, up3=up3, c_upcast=cast_cnt[tag_up], si_=si_, gcol=gcol):
                        W(q, "sy", GPc, c_upcast)
                        W(q, "sy", PEc, w13_rel[slot])
                        W(q, "sy", VEc, ffn_gate_ve)
                        dst = w13_sb[:, slot * CC * 256:(slot + 1) * CC * 256] \
                            .rearrange("p (cc o) -> p cc o", o=256)
                        q.dma_start(dst,
                                    up3[:, :, gcol + ic * 256: gcol + (ic + 1) * 256]) \
                            .then_inc(SYc.h, 16)
                        SYc.n += 16
                        w13_rdy[(th, si_, ic)] = SYc.n

                    blk.sync(wl)

                    def upmm(t, ic=ic, slot=slot, si_=si_, t0=t0):
                        W(t, "pe", SYc, w13_rdy[(th, si_, ic)])
                        if ic == 0:
                            W(t, "pe", VEc)
                            W(t, "pe", SEc)
                        if ic >= 2:
                            W(t, "pe", SEc, my.get(("sil", ic - 2), 0))
                            W(t, "pe", VEc, my.get(("h1", ic - 2), 0))
                        lastu = None
                        for cc in range(CC):
                            lg_ = w13_sb[:, slot * CC * 256 + cc * 256: slot * CC * 256 + cc * 256 + 128]
                            lu_ = w13_sb[:, slot * CC * 256 + cc * 256 + 128: slot * CC * 256 + (cc + 1) * 256]
                            for s0 in range(0, HW, 512):
                                rhs = xb[:, cc * T + t0 + s0: cc * T + t0 + s0 + 512]
                                t.matmul(pg[ic % 2][:, s0:s0 + 512], lg_, rhs,
                                         start=(cc == 0), stop=(cc == CC - 1))
                                lastu = t.matmul(pu[ic % 2][:, s0:s0 + 512], lu_, rhs,
                                                 start=(cc == 0), stop=(cc == CC - 1))
                        PEc.inc(lastu)
                        w13_rel[slot] = PEc.n

                    blk.tensor(upmm)
                    c_up = PEc.n

                    def silf(sc, ic=ic, c_up=c_up):
                        W(sc, "se", PEc, c_up)
                        if ic >= 1:
                            W(sc, "se", VEc, my.get(("h", ic - 1), 0))
                        si(sc.activation(silg[:, :], pg[ic % 2][:, :], AF.Silu))
                        my[("sil", ic)] = SEc.n

                    blk.scalar(silf)

                    def h1f(v, ic=ic, c_up=c_up, si_=si_, th=th):
                        W(v, "ve", PEc, c_up)
                        W(v, "ve", VEc, c_bca[0])
                        if si_ > 0 or th > 0:
                            W(v, "ve", PEc, c_pd_last[0])
                        if si_ == 0 and ic == 0:
                            W(v, "ve", SEc)
                        my[("h1", ic)] = vi(v.tensor_mul(h1s[:, :], pu[ic % 2][:, :], bca[:, :]))

                    blk.vector(h1f)

                    def hf(v, ic=ic):
                        W(v, "ve", SEc, my[("sil", ic)])
                        W(v, "ve", VEc, my[("h1", ic)])
                        my[("h", ic)] = vi(v.tensor_mul(h_sb[:, ic * HW:(ic + 1) * HW],
                                                        h1s[:, :], silg[:, :]))

                    blk.vector(hf)

                c_hall = VEc.n
                es_up.close()

                es_dn = ExitStack()
                pd = [es_dn.enter_context(nc.psum_tensor(nm(f"pd{i}"), [128, HW], F32)) for i in range(2)]
                if si_ + 1 < len(specs_run):
                    es_bc = ExitStack()
                    pbb = es_bc.enter_context(nc.psum_tensor(nm("pbb"), [128, HW], F32))
                    emit_bcast_mm(si_ + 1, pbb)
                    es_bc.close()
                for cw in range(8):
                    slot = cw % 2

                    def w2l(q, cw=cw, slot=slot, dn3=dn3, c_dncast=cast_cnt[tag_dn], si_=si_):
                        W(q, "sy", GPc, c_dncast)
                        W(q, "sy", PEc, w2_rel[slot])
                        W(q, "sy", VEc, ffn_gate_ve)
                        dst = w2_sb[:, slot * NIC * 256:(slot + 1) * NIC * 256] \
                            .rearrange("p (ic o) -> p ic o", o=256)
                        q.dma_start(dst, dn3[:, :, cw * 256:(cw + 1) * 256]).then_inc(SYc.h, 16)
                        SYc.n += 16
                        w2_rdy[(th, si_, cw)] = SYc.n

                    blk.sync(w2l)

                    for ccl in range(2):
                        cc = cw * 2 + ccl
                        pbuf = cc % 2

                        def dmm(t, cc=cc, ccl=ccl, cw=cw, slot=slot, pbuf=pbuf, si_=si_, c_hall=c_hall):
                            W(t, "pe", SYc, w2_rdy[(th, si_, cw)])
                            W(t, "pe", VEc, c_hall)
                            if si_ == 0 and cc < 2:
                                W(t, "pe", SEc)
                                W(t, "pe", VEc)
                            W(t, "pe", VEc, pd_rel[pbuf])
                            last = None
                            for ic in range(NIC):
                                lhs = w2_sb[:, slot * NIC * 256 + ic * 256 + ccl * 128:
                                            slot * NIC * 256 + ic * 256 + ccl * 128 + 128]
                                for s0 in range(0, HW, 512):
                                    last = t.matmul(pd[pbuf][:, s0:s0 + 512], lhs,
                                                    h_sb[:, ic * HW + s0: ic * HW + s0 + 512],
                                                    start=(ic == 0), stop=(ic == NIC - 1))
                            PEc.inc(last)
                            w2_rel[slot] = PEc.n
                            c_pd_last[0] = PEc.n

                        blk.tensor(dmm)
                        c_pd = PEc.n

                        def accf(v, cc=cc, pbuf=pbuf, si_=si_, c_pd=c_pd):
                            W(v, "ve", PEc, c_pd)
                            if si_ == 0:
                                a = vi(v.tensor_copy(oh[:, cc * HW:(cc + 1) * HW], pd[pbuf][:, :]))
                            else:
                                a = vi(v.tensor_add(oh[:, cc * HW:(cc + 1) * HW],
                                                    oh[:, cc * HW:(cc + 1) * HW], pd[pbuf][:, :]))
                            pd_rel[pbuf] = a
                            c_accL[cc] = a

                        blk.vector(accf)
                es_dn.close()

                if "acc0" in TAPS and th == 0 and si_ == 0:
                    def tacc0(g):
                        W(g, "gp", VEc, max(c_accL.values()))
                        g.dma_start(taps["acc0"][:, :], oh[:, :]).then_inc(GPc.h, 16)
                        GPc.n += 16

                    blk.gpsimd(tacc0)
                    c_tap = GPc.n

                    def vwait(v, c_tap=c_tap):
                        W(v, "ve", GPc, c_tap)

                    blk.vector(vwait)

            if "accA" in TAPS and th == 0:
                def taccA(g):
                    W(g, "gp", VEc, max(c_accL.values()))
                    g.dma_start(taps["accA"][:, :], oh[:, :]).then_inc(GPc.h, 16)
                    GPc.n += 16

                blk.gpsimd(taccA)
            es_E.close()
            if STOP_AFTER == "ffnA" and th == 0:
                def fin2(g):
                    W(g, "gp", GPc)

                blk.gpsimd(fin2)
                return nc

            es_L = ExitStack()
            y2row2 = es_L.enter_context(nc.sbuf_tensor(nm("y2row"), [128, 2 * D], F32))
            p2row = h_sb[:, 0:CC * 512]
            p2tm = h_sb[:, CC * 512:CC * 512 + D]
            scrq = h_sb[:, CC * 512 + D:CC * 512 + D + 512]

            es_Lp = ExitStack()
            ptb2 = [es_Lp.enter_context(nc.psum_tensor(nm(f"ptb2_{i}"), [128, 512], BF16)) for i in range(2)]

            c_y2dma = [GPc.n]
            for grp in range(2):
                c_p2r = {}
                for cc in range(CC):
                    def p2f(v, cc=cc, grp=grp, t0=t0):
                        W(v, "ve", VEc, c_accL[cc])
                        W(v, "ve", PEc, c_pd_last[0])
                        c_p2r[cc] = vi(v.tensor_add(p2row[:, cc * 512:(cc + 1) * 512],
                                                    xb[:, cc * T + t0 + grp * 512: cc * T + t0 + grp * 512 + 512],
                                                    oh[:, cc * HW + grp * 512: cc * HW + grp * 512 + 512]))

                    blk.vector(p2f)

                c_ev4 = {}
                for tch in range(4):
                    for b4 in range(4):
                        def p2tr(t, tch=tch, b4=b4, grp=grp):
                            for cc in range(b4 * 4, b4 * 4 + 4):
                                W(t, "pe", VEc, c_p2r[cc])
                            W(t, "pe", SYc, const_cnt["identb"])
                            if (tch * 4 + b4) >= 2:
                                W(t, "pe", VEc, c_ev4[(tch * 4 + b4) - 2])
                            else:
                                W(t, "pe", VEc)
                                W(t, "pe", SEc)
                            last = None
                            for cc in range(b4 * 4, b4 * 4 + 4):
                                last = t.transpose(ptb2[b4 % 2][:, (cc % 4) * 128:((cc % 4) + 1) * 128],
                                                   p2row[:, cc * 512 + tch * 128: cc * 512 + (tch + 1) * 128],
                                                   identb[:, :])
                            PEc.inc(last)

                        blk.tensor(p2tr)
                        c_ptr = PEc.n

                        def p2ev(v, tch=tch, b4=b4, c_ptr=c_ptr):
                            W(v, "ve", PEc, c_ptr)
                            c_ev4[tch * 4 + b4] = vi(v.tensor_copy(p2tm[:, b4 * 512:(b4 + 1) * 512],
                                                                   ptb2[b4 % 2][:, :]))

                        blk.vector(p2ev)

                    c_tm = VEc.n

                    def ln2a(v, c_tm=c_tm):
                        W(v, "ve", VEc, c_tm)
                        vi(v.reduce_sum(mcol[:, 0:1], p2tm[:, :], AX.X))

                    blk.vector(ln2a)
                    c_mc = VEc.n

                    def ln2sq(sc, c_tm=c_tm):
                        W(sc, "se", VEc, c_tm)
                        for k in range(4):
                            si(sc.activation(scrq[:, :], p2tm[:, k * 512:(k + 1) * 512], AF.Square,
                                             accum_out=sscol[:, k:k + 1]))

                    blk.scalar(ln2sq)
                    c_ss = SEc.n

                    def ln2b(v, c_mc=c_mc, c_ss=c_ss):
                        W(v, "ve", VEc, c_mc)
                        W(v, "ve", SEc, c_ss)
                        a = vi(v.reduce_sum(tmpcol[:, 0:1], sscol[:, :], AX.X))
                        W(v, "ve", VEc, a)
                        a = vi(v.tensor_scalar_mul(mcol[:, 0:1], mcol[:, 0:1], 1.0 / D))
                        a = vi(v.tensor_scalar_mul(tmpcol[:, 0:1], tmpcol[:, 0:1], 1.0 / D))
                        W(v, "ve", VEc, a)
                        a = vi(v.tensor_mul(rstdc[:, 0:1], mcol[:, 0:1], mcol[:, 0:1]))
                        W(v, "ve", VEc, a)
                        a = vi(v.tensor_sub(rstdc[:, 0:1], tmpcol[:, 0:1], rstdc[:, 0:1]))

                    blk.vector(ln2b)
                    c_v1 = VEc.n

                    def ln2c(sc, c_v1=c_v1):
                        W(sc, "se", VEc, c_v1)
                        W(sc, "se", VEc, const_cnt["cols"])
                        si(sc.activation(rstdc[:, 0:1], rstdc[:, 0:1], AF.Sqrt, bias=ceps[:, 0:1]))

                    blk.scalar(ln2c)
                    c_v2 = SEc.n

                    def ln2d(v, c_v2=c_v2):
                        W(v, "ve", SEc, c_v2)
                        a = vi(v.reciprocal(rstdc[:, 0:1], rstdc[:, 0:1]))
                        W(v, "ve", VEc, a)
                        a = vi(v.scalar_tensor_tensor(bcol[:, 0:1], mcol[:, 0:1], -1.0,
                                                      rstdc[:, 0:1], ALU.mult, ALU.mult))
                        W(v, "ve", VEc, a)
                        W(v, "ve", GPc, c_y2dma[0])
                        ysl = ((grp * 4 + tch) % 2) * D
                        vi(v.tensor_scalar(y2row2[:, ysl:ysl + D], p2tm[:, :], rstdc[:, 0:1],
                                           bcol[:, 0:1], ALU.mult, ALU.add))

                    blk.vector(ln2d)
                    c_y2 = VEc.n

                    def outd(g, grp=grp, tch=tch, t0=t0, c_y2=c_y2):
                        W(g, "gp", VEc, c_y2)
                        r0 = t0 + grp * 512 + tch * 128
                        ysl = ((grp * 4 + tch) % 2) * D
                        g.dma_start(y_out[r0:r0 + 128, :], y2row2[:, ysl:ysl + D]).then_inc(GPc.h, 16)
                        GPc.n += 16
                        if (grp * 4 + tch) % 2 == 1:
                            c_y2dma[0] = GPc.n

                    blk.gpsimd(outd)

            es_Lp.close()
            es_L.close()

        def fin(g):
            W(g, "gp", GPc)

        blk.gpsimd(fin)

    assert not cast_queue, "cast queue not fully pumped"
    return nc


def _prep_inputs(inputs):
    asc = np.ascontiguousarray
    f32 = np.float32
    wqkv = np.asarray(inputs["L0_attn_w_qkv"], dtype=f32)
    wout = np.asarray(inputs["L0_attn_w_out"], dtype=f32)
    gate = np.asarray(inputs["L0_ffn_gate_w"], dtype=f32)
    sgate = np.asarray(inputs["L0_ffn_sgate_w"], dtype=f32)
    w13 = np.asarray(inputs["L0_ffn_w13"], dtype=f32)
    w2 = np.asarray(inputs["L0_ffn_w2"], dtype=f32)
    sgu = np.asarray(inputs["L0_ffn_shared_gate_up_w"], dtype=f32)
    sdown = np.asarray(inputs["L0_ffn_shared_down_w"], dtype=f32)
    gsgT = np.zeros((D, 32), dtype=f32)
    gsgT[:, 0:16] = gate.T
    gsgT[:, 16] = sgate[0]
    def il_w13(wt):
        g = wt[:, 0:DI].reshape(D, NIC, 128)
        u = wt[:, DI:2 * DI].reshape(D, NIC, 128)
        return np.stack([g, u], axis=2).reshape(D, 2 * DI)

    sguT_ = sgu.T
    sgu_groups = []
    for gi in range(4):
        gg = sguT_[:, gi * DI:(gi + 1) * DI].reshape(D, NIC, 128)
        uu = sguT_[:, DIS + gi * DI:DIS + (gi + 1) * DI].reshape(D, NIC, 128)
        sgu_groups.append(np.stack([gg, uu], axis=2).reshape(D, 2 * DI))
    sguT_il = asc(np.concatenate(sgu_groups, axis=1))
    w13T_il = asc(np.stack([il_w13(np.asarray(w13[e].T)) for e in range(NEXP)], axis=0))
    base = {
        "wqkvT": asc(wqkv.T),
        "woutT": asc(wout.T),
        "gsgT": gsgT,
        "sguT": sguT_il,
        "sdownT": asc(sdown.T),
        "w13T": w13T_il,
        "w2T": asc(w2.transpose(0, 2, 1)),
    }
    x = np.asarray(inputs["L0_attn_x"], dtype=f32)
    return base, x


def kernel(**inputs):
    base, x = _prep_inputs(inputs)
    nc = build()
    in_maps = []
    for b in range(N_CORES):
        m = dict(base)
        m["xT"] = np.ascontiguousarray(x[b].T)
        in_maps.append(m)
    res = run_bass_kernel_spmd(nc, in_maps, core_ids=list(range(N_CORES)))
    return np.stack([r["y_out"] for r in res.results], axis=0)


# version 09
# speedup vs baseline: 1.2018x; 1.2018x over previous
import numpy as np
from contextlib import ExitStack

import concourse.bass as bass
import concourse.mybir as mybir
from concourse.bass_utils import run_bass_kernel_spmd

F32 = mybir.dt.float32
BF16 = mybir.dt.bfloat16
AF = mybir.ActivationFunctionType
ALU = mybir.AluOpType
AX = mybir.AxisListType

T = 2048
D = 2048
CC = 16
NH = 16
DH = 128
ROT = 32
HW = 1024
NEXP = 16
DI = 1408
NIC = 11
DIS = 5632
QSCALE = DH ** -0.5
EXPB = -4.0
NEG = -1e30

N_CORES = 8
TAPS = set()
STOP_AFTER = None
TAP_HEAD = 0
DEV_WAIT_CASTS = False
DEV_N_EXPERTS = None


def _host_consts():
    import ml_dtypes
    inv_freq = 10000.0 ** (-(np.arange(0, ROT, 2, dtype=np.float64) / ROT))
    tt = np.arange(T, dtype=np.float64)[:, None] * inv_freq[None, :]
    ang = np.concatenate([tt, tt], axis=1).T
    csin = np.concatenate([np.cos(ang), np.sin(ang)], axis=0)
    csin_bf = np.asarray(csin.astype(ml_dtypes.bfloat16))
    P = np.zeros((ROT, ROT), dtype=np.float64)
    for dp in range(16):
        P[dp + 16, dp] = -1.0
        P[dp, dp + 16] = 1.0
    P_bf = np.asarray(P.astype(ml_dtypes.bfloat16))
    tri = np.where(np.arange(128)[:, None] > np.arange(128)[None, :], NEG, 0.0).astype(np.float32)
    identb = np.asarray(np.eye(128).astype(ml_dtypes.bfloat16))
    identf = np.eye(128, dtype=np.float32)
    ones_p = np.asarray(np.ones((128, 1)).astype(ml_dtypes.bfloat16))
    ones_rf = np.ones((1, 128), dtype=np.float32)
    ones_rb = np.asarray(np.ones((1, 128)).astype(ml_dtypes.bfloat16))
    return csin_bf, P_bf, tri, identb, identf, ones_p, ones_rf, ones_rb


class Ctr:
    def __init__(self, nc, name):
        self.h = nc.alloc_semaphore(name)
        self.n = 0

    def inc(self, instr, k=1):
        instr.then_inc(self.h, k)
        self.n += k
        return self.n


_WSEEN = {}


def W(eng, key, ctr, at=None):
    at = ctr.n if at is None else at
    assert at <= ctr.n, "wait on future count (emission-order bug)"
    if at <= 0:
        return
    k = (key, id(ctr))
    if _WSEEN.get(k, -1) >= at:
        return
    _WSEEN[k] = at
    eng.wait_ge(ctr.h, at)


def build():
    _WSEEN.clear()
    nc = bass.Bass("TRN2", target_bir_lowering=False)
    _uid = [0]

    def nm(b):
        _uid[0] += 1
        return f"{b}_{_uid[0]}"

    xT = nc.dram_tensor("xT", [D, T], F32, kind="ExternalInput").ap()
    wqkvT = nc.dram_tensor("wqkvT", [D, 3 * D], F32, kind="ExternalInput").ap()
    woutT = nc.dram_tensor("woutT", [D, D], F32, kind="ExternalInput").ap()
    gsgT = nc.dram_tensor("gsgT", [D, 32], F32, kind="ExternalInput").ap()
    sguT = nc.dram_tensor("sguT", [D, 2 * DIS], F32, kind="ExternalInput").ap()
    sdownT = nc.dram_tensor("sdownT", [DIS, D], F32, kind="ExternalInput").ap()
    w13T = nc.dram_tensor("w13T", [NEXP, D, 2 * DI], F32, kind="ExternalInput").ap()
    w2T = nc.dram_tensor("w2T", [NEXP, DI, D], F32, kind="ExternalInput").ap()
    y_out = nc.dram_tensor("y_out", [T, D], F32, kind="ExternalOutput").ap()

    wqkvT_bf = nc.dram_tensor("wqkvT_bf", [D, 3 * D], BF16, kind="Internal").ap()
    woutT_bf = nc.dram_tensor("woutT_bf", [D, D], BF16, kind="Internal").ap()
    sguT_bf = nc.dram_tensor("sguT_bf", [D, 2 * DIS], BF16, kind="Internal").ap()
    sdownT_bf = nc.dram_tensor("sdownT_bf", [DIS, D], BF16, kind="Internal").ap()
    w13T_bf = nc.dram_tensor("w13T_bf", [NEXP, D, 2 * DI], BF16, kind="Internal").ap()
    w2T_bf = nc.dram_tensor("w2T_bf", [NEXP, DI, D], BF16, kind="Internal").ap()

    csin_np, P_np, tri_np, identb_np, identf_np, onesp_np, onesrf_np, onesrb_np = _host_consts()
    csin_d = nc.inline_tensor(csin_np, "csin_d").ap()
    prot_d = nc.inline_tensor(P_np, "prot_d").ap()
    tri_d = nc.inline_tensor(tri_np, "tri_d").ap()
    identb_d = nc.inline_tensor(identb_np, "identb_d").ap()
    identf_d = nc.inline_tensor(identf_np, "identf_d").ap()
    onesp_d = nc.inline_tensor(onesp_np, "onesp_d").ap()
    onesrf_d = nc.inline_tensor(onesrf_np, "onesrf_d").ap()
    onesrb_d = nc.inline_tensor(onesrb_np, "onesrb_d").ap()

    taps = {}
    tapshape = {"qkv0": [3 * 128, T], "oh1": [128, NH * HW], "y1": [128, CC * T],
                "coef0": [32, HW], "acc0": [128, CC * HW], "accA": [128, CC * HW],
                "wsb": [128, CC * 384], "qev": [128, HW], "qro": [128, HW], "qed": [128, HW], "ked": [128, T]}
    for t_ in TAPS:
        taps[t_] = nc.dram_tensor(f"tap_{t_}", tapshape[t_], BF16, kind="ExternalOutput").ap()

    xb = nc.alloc_sbuf_tensor("xb", [128, CC * T], BF16)
    oh = nc.alloc_sbuf_tensor("oh", [128, NH * HW], BF16)
    gsg = nc.alloc_sbuf_tensor("gsg", [128, CC * 32], BF16)
    identb = nc.alloc_sbuf_tensor("identb", [128, 128], BF16)
    identf = nc.alloc_sbuf_tensor("identf", [128, 128], F32)
    ones_p = nc.alloc_sbuf_tensor("ones_p", [128, 1], BF16)
    ones_rf = nc.alloc_sbuf_tensor("ones_rf", [1, 128], F32)
    ones_rb = nc.alloc_sbuf_tensor("ones_rb", [1, 128], BF16)
    rowA = nc.alloc_sbuf_tensor("rowA", [1, HW], F32)
    rowB = nc.alloc_sbuf_tensor("rowB", [1, HW], F32)
    mcol = nc.alloc_sbuf_tensor("mcol", [128, 1], F32)
    sscol = nc.alloc_sbuf_tensor("sscol", [128, 4], F32)
    tmpcol = nc.alloc_sbuf_tensor("tmpcol", [128, 1], F32)
    rstdc = nc.alloc_sbuf_tensor("rstdc", [128, 1], F32)
    bcol = nc.alloc_sbuf_tensor("bcol", [128, 1], F32)
    cexpb = nc.alloc_sbuf_tensor("cexpb", [128, 1], F32)
    ceps = nc.alloc_sbuf_tensor("ceps", [128, 1], F32)

    PEc = Ctr(nc, "PEc")
    VEc = Ctr(nc, "VEc")
    SEc = Ctr(nc, "SEc")
    SYc = Ctr(nc, "SYc")
    GPc = Ctr(nc, "GPc")

    def vi(instr):
        return VEc.inc(instr)

    def si(instr):
        return SEc.inc(instr)

    cast_cnt = {}
    const_cnt = {}
    cast_gate_pe = [30]
    cast_queue = []
    for i in range(16):
        cast_queue.append((sguT_bf[i * 128:(i + 1) * 128, :], sguT[i * 128:(i + 1) * 128, :],
                           "sgu" if i == 15 else None))
    for i in range(8):
        cast_queue.append((sdownT_bf[i * 704:(i + 1) * 704, :], sdownT[i * 704:(i + 1) * 704, :],
                           "sdown" if i == 7 else None))
    for e in range(NEXP):
        for i in range(4):
            cast_queue.append((w13T_bf[e][i * 512:(i + 1) * 512, :], w13T[e][i * 512:(i + 1) * 512, :],
                               f"w13_{e}" if i == 3 else None))
        for i in range(2):
            cast_queue.append((w2T_bf[e][i * 704:(i + 1) * 704, :], w2T[e][i * 704:(i + 1) * 704, :],
                               f"w2_{e}" if i == 1 else None))

    with nc.Block() as blk:
        def g_casts(g):
            for i in range(2):
                g.dma_start(wqkvT_bf[i * 1024:(i + 1) * 1024, :],
                            wqkvT[i * 1024:(i + 1) * 1024, :]).then_inc(GPc.h, 16)
                GPc.n += 16
            cast_cnt["wqkv"] = GPc.n
            g.dma_start(xb[:, :].rearrange("p (cc t) -> p cc t", t=T),
                        xT.rearrange("(cc p) t -> p cc t", p=128)).then_inc(GPc.h, 16)
            GPc.n += 16
            cast_cnt["xb"] = GPc.n
            g.wait_ge(PEc.h, cast_gate_pe[0])
            for i in range(2):
                g.dma_start(woutT_bf[i * 1024:(i + 1) * 1024, :],
                            woutT[i * 1024:(i + 1) * 1024, :]).then_inc(GPc.h, 16)
                GPc.n += 16
            cast_cnt["wout"] = GPc.n
            g.dma_start(gsg[:, :].rearrange("p (cc k) -> p cc k", k=32),
                        gsgT.rearrange("(cc p) k -> p cc k", k=32, p=128)).then_inc(GPc.h, 16)
            GPc.n += 16
            cast_cnt["gsg"] = GPc.n
            pass

        blk.gpsimd(g_casts)

        def cast_pump(n):
            if not cast_queue:
                return
            items = [cast_queue.pop(0) for _ in range(min(n, len(cast_queue)))]
            gate = PEc.n

            def f(g):
                if gate > 0:
                    g.wait_ge(PEc.h, gate)
                for dst, srcx, tag in items:
                    g.dma_start(dst, srcx).then_inc(GPc.h, 16)
                    GPc.n += 16
                    if tag is not None:
                        cast_cnt[tag] = GPc.n

            blk.gpsimd(f)

        if STOP_AFTER is not None:
            cast_pump(10 ** 6)

        def s_consts(s):
            for name, dst, src in (("identb", identb, identb_d), ("identf", identf, identf_d),
                                   ("ones_p", ones_p, onesp_d), ("ones_rf", ones_rf, onesrf_d),
                                   ("ones_rb", ones_rb, onesrb_d)):
                s.dma_start(dst[:, :], src[:, :]).then_inc(SYc.h, 16)
                SYc.n += 16
                const_cnt[name] = SYc.n

        blk.sync(s_consts)

        def v_consts(v):
            vi(v.memset(cexpb[:, :], EXPB))
            vi(v.memset(ceps[:, :], 1e-5))

        blk.vector(v_consts)
        const_cnt["cols"] = VEc.n

        es_A = ExitStack()
        csin = es_A.enter_context(nc.sbuf_tensor(nm("csin"), [64, T], BF16))
        prot = es_A.enter_context(nc.sbuf_tensor(nm("prot"), [32, 32], BF16))
        tri = es_A.enter_context(nc.sbuf_tensor(nm("tri"), [128, 128], F32))
        wqkv_sb = es_A.enter_context(nc.sbuf_tensor(nm("wqkv_sb"), [128, 2 * 3 * CC * 256], BF16))

        def s_consts2(s):
            for name, dst, src in (("csin", csin, csin_d), ("prot", prot, prot_d), ("tri", tri, tri_d)):
                s.dma_start(dst[:, :], src[:, :]).then_inc(SYc.h, 16)
                SYc.n += 16
                const_cnt[name] = SYc.n

        blk.sync(s_consts2)

        wqkv_rel = [0, 0]
        wqkv_rdy = {}
        c_oh = {}
        c_y1 = [0] * CC

        def prefetch_pair(half, hp):
            buf = hp % 2

            def f(q):
                W(q, "sy", GPc, cast_cnt["wqkv"])
                W(q, "sy", PEc, wqkv_rel[buf])
                for role in range(3):
                    dstb = wqkv_sb[:, buf * 3 * CC * 256 + role * CC * 256:
                                   buf * 3 * CC * 256 + (role + 1) * CC * 256] \
                        .rearrange("p (cc o) -> p cc o", o=256)
                    src = wqkvT_bf[:, role * D + hp * 256: role * D + (hp + 1) * 256]
                    q.dma_start(dstb, src.rearrange("(cc p) o -> p cc o", p=128)).then_inc(SYc.h, 16)
                    SYc.n += 16
                wqkv_rdy[(half, hp)] = SYc.n

            blk.sync(f)

        def qkv_matmul(psum, role, width, t0, h):
            buf = (h // 2) % 2
            sub = h % 2

            def f(t):
                W(t, "pe", GPc, cast_cnt["w2_15"] if DEV_WAIT_CASTS else cast_cnt["xb"])
                W(t, "pe", VEc)
                W(t, "pe", SEc)
                last = None
                for cc in range(CC):
                    base = buf * 3 * CC * 256 + role * CC * 256 + cc * 256 + sub * 128
                    lhs = wqkv_sb[:, base: base + 128]
                    for s0 in range(0, width, 512):
                        last = t.matmul(psum[:, s0:s0 + 512], lhs,
                                        xb[:, cc * T + t0 + s0: cc * T + t0 + s0 + 512],
                                        start=(cc == 0), stop=(cc == CC - 1))
                PEc.inc(last)

            return f

        for half in (1, 0):
            t0q = half * HW
            jbmax = (t0q + HW) // 128

            es_hscr = ExitStack()
            q_sb = es_hscr.enter_context(nc.sbuf_tensor(nm("q_sb"), [128, HW], BF16))
            k_sb = es_hscr.enter_context(nc.sbuf_tensor(nm("k_sb"), [128, T], BF16))
            vfm = es_hscr.enter_context(nc.sbuf_tensor(nm("vfm"), [128, T], BF16))
            vtm = es_hscr.enter_context(nc.sbuf_tensor(nm("vtm"), [128, T], BF16))
            Ebuf = es_hscr.enter_context(nc.sbuf_tensor(nm("Ebuf"), [128, 2 * HW], BF16))
            stscr = es_hscr.enter_context(nc.sbuf_tensor(nm("stscr"), [128, 128], F32))
            rb_sb = es_hscr.enter_context(nc.sbuf_tensor(nm("rb_sb"), [128, HW], BF16))
            rt1 = es_hscr.enter_context(nc.sbuf_tensor(nm("rt1"), [32, 512], BF16))
            rt2 = es_hscr.enter_context(nc.sbuf_tensor(nm("rt2"), [32, 512], BF16))

            def rotary(psum_rot, dst_sb, width, t0):
                for ch0 in range(0, width, 512):
                    w = min(512, width - ch0)

                    def tmm(t, ch0=ch0, w=w):
                        W(t, "pe", SYc, const_cnt["prot"])
                        W(t, "pe", SEc)
                        W(t, "pe", VEc)
                        PEc.inc(t.matmul(psum_rot[0:32, 0:w], prot[0:32, 0:32],
                                         dst_sb[0:32, ch0:ch0 + w], start=True, stop=True))

                    blk.tensor(tmm)
                    c_rot = PEc.n

                    def vcomb(v, ch0=ch0, w=w, c_rot=c_rot):
                        W(v, "ve", SYc, const_cnt["csin"])
                        W(v, "ve", PEc, c_rot)
                        vi(v.tensor_mul(rt1[0:32, 0:w], dst_sb[0:32, ch0:ch0 + w],
                                        csin[0:32, t0 + ch0:t0 + ch0 + w]))
                        b = vi(v.tensor_mul(rt2[0:32, 0:w], psum_rot[0:32, 0:w],
                                            csin[32:64, t0 + ch0:t0 + ch0 + w]))
                        W(v, "ve", VEc, b)
                        vi(v.tensor_add(dst_sb[0:32, ch0:ch0 + w], rt1[0:32, 0:w], rt2[0:32, 0:w]))

                    blk.vector(vcomb)

            g_se = [SEc.n]
            g_ve_q = [VEc.n]
            g_ve_kv = [VEc.n]
            c_rb = [0]
            prefetch_pair(half, 0)
            for h in range(NH):
                cast_pump(2)
                if h % 2 == 0 and h // 2 + 1 < NH // 2:
                    prefetch_pair(half, h // 2 + 1)

                es_head = ExitStack()
                pav = es_head.enter_context(nc.psum_tensor(nm("pav"), [128, HW], F32))

                kw = jbmax * 128
                ngrp = kw // 512
                es_qkv = ExitStack()
                pq = es_qkv.enter_context(nc.psum_tensor(nm("pq"), [128, HW], F32))
                prr = es_qkv.enter_context(nc.psum_tensor(nm("prr"), [128, 512], F32))
                pkv = es_qkv.enter_context(nc.psum_tensor(nm("pkv"), [128, 1024], F32))
                ptb = es_qkv.enter_context(nc.psum_tensor(nm("ptb"), [128, 512], BF16))

                def wait_w(t, half=half, h=h):
                    W(t, "pe", SYc, wqkv_rdy[(half, h // 2)])

                blk.tensor(wait_w)

                def qmm(t, h=h, t0q=t0q, gse=g_se[0], gveq=g_ve_q[0]):
                    W(t, "pe", GPc, cast_cnt["xb"])
                    W(t, "pe", SEc, gse)
                    W(t, "pe", VEc, gveq)
                    buf = (h // 2) % 2
                    sub = h % 2
                    last = None
                    for cc in range(CC):
                        base = buf * 3 * CC * 256 + 0 * CC * 256 + cc * 256 + sub * 128
                        lhs = wqkv_sb[:, base: base + 128]
                        for s0 in range(0, HW, 512):
                            last = t.matmul(pq[:, s0:s0 + 512], lhs,
                                            xb[:, cc * T + t0q + s0: cc * T + t0q + s0 + 512],
                                            start=(cc == 0), stop=(cc == CC - 1))
                    PEc.inc(last)

                blk.tensor(qmm)
                c_pq = PEc.n

                def qevac(sc, c_pq=c_pq):
                    W(sc, "se", PEc, c_pq)
                    si(sc.activation(q_sb[:, :], pq[:, :], AF.Copy, scale=QSCALE))

                blk.scalar(qevac)
                c_qev = SEc.n

                c_kev = {}
                c_vev = {}
                for g4 in range(ngrp):
                    def kmm(t, g4=g4, h=h, gvekv=g_ve_kv[0], gse=g_se[0]):
                        if g4 == 0:
                            W(t, "pe", VEc, gvekv)
                            W(t, "pe", SEc, gse)
                        if g4 >= 2:
                            W(t, "pe", SEc, c_kev[g4 - 2])
                        buf = (h // 2) % 2
                        sub = h % 2
                        last = None
                        for cc in range(CC):
                            base = buf * 3 * CC * 256 + 1 * CC * 256 + cc * 256 + sub * 128
                            lhs = wqkv_sb[:, base: base + 128]
                            last = t.matmul(pkv[:, (g4 % 2) * 512:(g4 % 2 + 1) * 512], lhs,
                                            xb[:, cc * T + g4 * 512: cc * T + (g4 + 1) * 512],
                                            start=(cc == 0), stop=(cc == CC - 1))
                        PEc.inc(last)

                    blk.tensor(kmm)
                    c_k = PEc.n

                    def kev(sc, g4=g4, c_k=c_k):
                        W(sc, "se", PEc, c_k)
                        c_kev[g4] = si(sc.activation(k_sb[:, g4 * 512:(g4 + 1) * 512],
                                                     pkv[:, (g4 % 2) * 512:(g4 % 2 + 1) * 512], AF.Copy))

                    blk.scalar(kev)

                rot_chunks = [(q_sb, 0, t0q, ("q", 0)), (q_sb, 512, t0q + 512, ("q", 0))] + \
                             [(k_sb, g * 512, g * 512, ("k", g)) for g in range(ngrp)]
                c_t2 = {}
                c_radd = {}

                def rot_pmm(idx):
                    dst, ch0, tb, dep = rot_chunks[idx]

                    def f(t, idx=idx, dst=dst, ch0=ch0, dep=dep):
                        W(t, "pe", SYc, const_cnt["prot"])
                        W(t, "pe", SEc, c_qev if dep[0] == "q" else c_kev[dep[1]])
                        if idx >= 1:
                            W(t, "pe", VEc, c_t2.get(idx - 1, 0))
                        PEc.inc(t.matmul(prr[0:32, 0:512], prot[0:32, 0:32],
                                         dst[0:32, ch0:ch0 + 512], start=True, stop=True))

                    blk.tensor(f)
                    c_radd[("pmm", idx)] = PEc.n

                def rot_comb(idx):
                    dst, ch0, tb, dep = rot_chunks[idx]

                    def f(v, idx=idx, dst=dst, ch0=ch0, tb=tb):
                        W(v, "ve", SYc, const_cnt["csin"])
                        W(v, "ve", PEc, c_radd[("pmm", idx)])
                        W(v, "ve", SEc, c_qev if rot_chunks[idx][3][0] == "q" else c_kev[rot_chunks[idx][3][1]])
                        vi(v.tensor_mul(rt1[0:32, :], dst[0:32, ch0:ch0 + 512],
                                        csin[0:32, tb:tb + 512]))
                        b = vi(v.tensor_mul(rt2[0:32, :], prr[0:32, 0:512],
                                            csin[32:64, tb:tb + 512]))
                        c_t2[idx] = b
                        W(v, "ve", VEc, b)
                        c_radd[("add", idx)] = vi(v.tensor_add(dst[0:32, ch0:ch0 + 512],
                                                               rt1[0:32, :], rt2[0:32, :]))

                    blk.vector(f)

                rot_pmm(0)
                rot_comb(0)
                rot_next = [1]

                last_vmm_pe = [0]
                for g4 in range(ngrp):
                    def vmm(t, g4=g4, h=h, ngrp=ngrp):
                        if g4 < 2:
                            W(t, "pe", SEc, c_kev[ngrp - 2 + g4])
                        else:
                            W(t, "pe", SEc, c_vev[g4 - 2])
                        buf = (h // 2) % 2
                        sub = h % 2
                        last = None
                        for cc in range(CC):
                            base = buf * 3 * CC * 256 + 2 * CC * 256 + cc * 256 + sub * 128
                            lhs = wqkv_sb[:, base: base + 128]
                            last = t.matmul(pkv[:, (g4 % 2) * 512:(g4 % 2 + 1) * 512], lhs,
                                            xb[:, cc * T + g4 * 512: cc * T + (g4 + 1) * 512],
                                            start=(cc == 0), stop=(cc == CC - 1))
                        PEc.inc(last)
                        last_vmm_pe[0] = PEc.n

                    blk.tensor(vmm)
                    c_v = PEc.n

                    def vev(sc, g4=g4, c_v=c_v):
                        W(sc, "se", PEc, c_v)
                        c_vev[g4] = si(sc.activation(vfm[:, g4 * 512:(g4 + 1) * 512],
                                                     pkv[:, (g4 % 2) * 512:(g4 % 2 + 1) * 512], AF.Copy))

                    blk.scalar(vev)
                    if rot_next[0] < len(rot_chunks):
                        rot_pmm(rot_next[0])
                        rot_comb(rot_next[0])
                        rot_next[0] += 1

                if h % 2 == 1:
                    wqkv_rel[(h // 2) % 2] = last_vmm_pe[0]

                c_vt_ev = {}
                for grp in range(ngrp):
                    def vtr(t, grp=grp, gvekv=g_ve_kv[0]):
                        W(t, "pe", SYc, const_cnt["identb"])
                        W(t, "pe", SEc, c_vev[grp])
                        W(t, "pe", VEc, c_vt_ev.get(grp - 1, gvekv))
                        last = None
                        for jb in range(grp * 4, grp * 4 + 4):
                            last = t.transpose(ptb[:, (jb % 4) * 128:((jb % 4) + 1) * 128],
                                               vfm[:, jb * 128:(jb + 1) * 128], identb[:, :])
                        PEc.inc(last)

                    blk.tensor(vtr)
                    c_tb = PEc.n

                    def vte(v, grp=grp, c_tb=c_tb):
                        W(v, "ve", PEc, c_tb)
                        c_vt_ev[grp] = vi(v.tensor_copy(vtm[:, grp * 512:(grp + 1) * 512],
                                                        ptb[:, :]))

                    blk.vector(vte)
                    if rot_next[0] < len(rot_chunks):
                        rot_pmm(rot_next[0])
                        rot_comb(rot_next[0])
                        rot_next[0] += 1
                while rot_next[0] < len(rot_chunks):
                    rot_pmm(rot_next[0])
                    rot_comb(rot_next[0])
                    rot_next[0] += 1
                c_vtm = VEc.n
                c_qrot = max(c_radd[("add", i)] for i in range(len(rot_chunks)))
                c_krot = c_qrot
                es_qkv.close()

                es_jb = ExitStack()
                pst = es_jb.enter_context(nc.psum_tensor(nm("pst"), [128, 2 * HW], F32))
                pden = es_jb.enter_context(nc.psum_tensor(nm("pden"), [128, 512], F32))
                pden2 = es_jb.enter_context(nc.psum_tensor(nm("pden2"), [128, 512], F32))
                c_st = {}
                c_exp = {}
                c_av = {}
                jbA_last = t0q // 128 + 3

                def emit_dav(jb, region):
                    il = max(0, jb * 128 - t0q)
                    slot = jb % 2
                    if region == "A" and il >= 512:
                        return

                    def dav(t, jb=jb, il=il, slot=slot, region=region):
                        W(t, "pe", SEc, c_exp[(jb, region)])
                        W(t, "pe", SYc, const_cnt["ones_p"])
                        if jb == 0:
                            W(t, "pe", VEc, c_vtm)
                        st_ = (jb == 0)
                        last = None
                        if region == "A":
                            spA = (jb == min(jbA_last, jbmax - 1))
                            t.matmul(pden[0:1, il:512], ones_p[:, 0:1],
                                     Ebuf[:, slot * HW + il: slot * HW + 512], start=st_, stop=spA)
                            last = t.matmul(pav[:, il:512], vtm[:, jb * 128:(jb + 1) * 128],
                                            Ebuf[:, slot * HW + il: slot * HW + 512], start=st_, stop=spA)
                        else:
                            spB = (jb == jbmax - 1)
                            lo2 = max(il, 512)
                            t.matmul(pden2[0:1, lo2 - 512:512], ones_p[:, 0:1],
                                     Ebuf[:, slot * HW + lo2: slot * HW + HW], start=st_, stop=spB)
                            last = t.matmul(pav[:, lo2:HW], vtm[:, jb * 128:(jb + 1) * 128],
                                            Ebuf[:, slot * HW + lo2: slot * HW + HW], start=st_, stop=spB)
                        c_av[(jb, region)] = PEc.inc(last)

                    blk.tensor(dav)

                for jb in range(jbmax):
                    il = max(0, jb * 128 - t0q)
                    slot = jb % 2
                    diag = (jb * 128 >= t0q)
                    diag_in_A = diag and il < 512

                    def st_mm(t, jb=jb, il=il, slot=slot):
                        W(t, "pe", VEc, max(c_qrot, c_krot))
                        if jb >= 2:
                            W(t, "pe", SEc, c_exp[(jb - 2, "B")])
                        else:
                            W(t, "pe", SEc)
                            W(t, "pe", VEc)
                        c_st[(jb, "A")] = 0
                        for s0 in range(0, HW, 512):
                            if s0 + 512 <= il:
                                continue
                            lo = max(s0, il)
                            mm = t.matmul(pst[:, slot * HW + lo: slot * HW + s0 + 512],
                                          k_sb[:, jb * 128:(jb + 1) * 128],
                                          q_sb[:, lo:s0 + 512], start=True, stop=True)
                            c_st[(jb, "A" if s0 == 0 else "B")] = PEc.inc(mm)

                    blk.tensor(st_mm)

                    if diag:
                        def mask_add(v, jb=jb, il=il, slot=slot):
                            W(v, "ve", PEc, c_st[(jb, "A" if il < 512 else "B")])
                            W(v, "ve", SYc, const_cnt["tri"])
                            vi(v.tensor_add(pst[:, slot * HW + il: slot * HW + il + 128],
                                            pst[:, slot * HW + il: slot * HW + il + 128], tri[:, :]))

                        blk.vector(mask_add)
                        c_mask = VEc.n
                    else:
                        c_mask = 0

                    def expf(sc, jb=jb, il=il, slot=slot, diag_in_A=diag_in_A, diag=diag, c_mask=c_mask):
                        W(sc, "se", VEc, const_cnt["cols"])
                        if jb >= 2:
                            W(sc, "se", PEc, c_av.get((jb - 2, "B"), 0))
                        if il < 512:
                            W(sc, "se", PEc, c_st[(jb, "A")])
                            if diag_in_A:
                                W(sc, "se", VEc, c_mask)
                            c_exp[(jb, "A")] = si(sc.activation(
                                Ebuf[:, slot * HW + il: slot * HW + 512],
                                pst[:, slot * HW + il: slot * HW + 512],
                                AF.Exp, bias=cexpb[:, 0:1]))
                        lo2 = max(il, 512)
                        W(sc, "se", PEc, c_st[(jb, "B")])
                        if diag and not diag_in_A:
                            W(sc, "se", VEc, c_mask)
                        c_exp[(jb, "B")] = si(sc.activation(
                            Ebuf[:, slot * HW + lo2: slot * HW + HW],
                            pst[:, slot * HW + lo2: slot * HW + HW],
                            AF.Exp, bias=cexpb[:, 0:1]))

                    blk.scalar(expf)

                    if jb >= 1:
                        emit_dav(jb - 1, "A")
                        emit_dav(jb - 1, "B")
                    if jb == jbmax - 1:
                        emit_dav(jb, "A")
                        emit_dav(jb, "B")

                if "qed" in TAPS and half == 1 and h == TAP_HEAD:
                    def tqed(g):
                        W(g, "gp", PEc, c_av[(jbmax - 1, "B")])
                        g.dma_start(taps["qed"][:, :], q_sb[:, :]).then_inc(GPc.h, 16)
                        GPc.n += 16
                        g.dma_start(taps["ked"][:, :], k_sb[:, :]).then_inc(GPc.h, 16)
                        GPc.n += 16

                    blk.gpsimd(tqed)
                    c_tqed = GPc.n

                    def tqed2(t, c_tqed=c_tqed):
                        W(t, "pe", GPc, c_tqed)

                    blk.tensor(tqed2)

                def recipd(v):
                    W(v, "ve", PEc, c_av[(jbmax - 1, "B")])
                    vi(v.reciprocal(rowA[0:1, 0:512], pden[0:1, 0:512]))
                    vi(v.reciprocal(rowA[0:1, 512:HW], pden2[0:1, 0:512]))

                blk.vector(recipd)
                c_rr = VEc.n
                es_jb.close()

                es_post = ExitStack()
                prb = es_post.enter_context(nc.psum_tensor(nm("prb"), [128, HW], F32))

                def bcast_r(t, c_rr=c_rr, jbmax=jbmax):
                    W(t, "pe", VEc, c_rr)
                    W(t, "pe", SEc, c_exp[(jbmax - 1, "B")])
                    W(t, "pe", SYc, const_cnt["ones_rf"])
                    last = None
                    for s0 in range(0, HW, 512):
                        last = t.matmul(prb[:, s0:s0 + 512], ones_rf[0:1, :],
                                        rowA[0:1, s0:s0 + 512], start=True, stop=True)
                    PEc.inc(last)

                blk.tensor(bcast_r)
                c_prb = PEc.n

                def ohev(v, h=h, half=half, c_prb=c_prb):
                    W(v, "ve", PEc, c_prb)
                    b = vi(v.tensor_copy(rb_sb[:, :], prb[:, :]))
                    c_rb[0] = b
                    W(v, "ve", VEc, b)
                    c_oh[(half, h)] = vi(v.tensor_mul(oh[:, h * HW:(h + 1) * HW], pav[:, :], rb_sb[:, :]))

                blk.vector(ohev)
                es_post.close()
                es_head.close()
                g_se[0] = c_exp[(jbmax - 1, "B")]
                g_ve_kv[0] = c_rr
                g_ve_q[0] = c_rb[0]

                if "qkv0" in TAPS and half == 1 and h == TAP_HEAD:
                    def tqkv(g):
                        W(g, "gp", VEc, c_oh[(1, TAP_HEAD)])
                        g.dma_start(taps["qkv0"][0:128, 0:HW], q_sb[:, :]).then_inc(GPc.h, 16)
                        GPc.n += 16
                        g.dma_start(taps["qkv0"][128:256, :], k_sb[:, :]).then_inc(GPc.h, 16)
                        GPc.n += 16
                        g.dma_start(taps["qkv0"][256:384, :], vtm[:, :]).then_inc(GPc.h, 16)
                        GPc.n += 16

                    blk.gpsimd(tqkv)
                    c_tapq = GPc.n

                    def twv(v, c_tapq=c_tapq):
                        W(v, "ve", GPc, c_tapq)

                    blk.vector(twv)

                    def tws(sc, c_tapq=c_tapq):
                        W(sc, "se", GPc, c_tapq)

                    blk.scalar(tws)

            if "oh1" in TAPS and half == 1:
                def toh(g):
                    W(g, "gp", VEc, c_oh[(1, NH - 1)])
                    g.dma_start(taps["oh1"][:, :], oh[:, :]).then_inc(GPc.h, 16)
                    GPc.n += 16

                blk.gpsimd(toh)

            es_hscr.close()

            es_w = ExitStack()
            wout_sb = es_w.enter_context(nc.sbuf_tensor(nm("wout_sb"), [128, 2 * CC * 256], BF16))
            sq_sb = es_w.enter_context(nc.sbuf_tensor(nm("sq_sb"), [128, 2 * HW], BF16))
            Abc = es_w.enter_context(nc.sbuf_tensor(nm("Abc"), [128, HW], F32))
            Bbc = es_w.enter_context(nc.sbuf_tensor(nm("Bbc"), [128, HW], F32))
            t1scr = es_w.enter_context(nc.sbuf_tensor(nm("t1scr"), [128, HW], F32))

            es_wp = ExitStack()
            pw = [es_wp.enter_context(nc.psum_tensor(nm(f"pw{i}"), [128, HW], F32)) for i in range(2)]
            psm = es_wp.enter_context(nc.psum_tensor(nm("psm"), [128, 512], F32))
            psm2 = es_wp.enter_context(nc.psum_tensor(nm("psm2"), [128, 512], F32))
            pss = es_wp.enter_context(nc.psum_tensor(nm("pss"), [128, 512], F32))
            pss2 = es_wp.enter_context(nc.psum_tensor(nm("pss2"), [128, 512], F32))

            wout_rel = [PEc.n, PEc.n]
            wout_gate_ve = VEc.n
            wout_rdy = {}
            c_p1 = {}
            c_sq = {}
            c_stat = {}
            for cc in range(CC):
                wbuf = (cc // 2) % 2

                if cc % 2 == 0:
                    def wload(q, cc=cc, wbuf=wbuf, wout_gate_ve=wout_gate_ve):
                        W(q, "sy", GPc, cast_cnt["wout"])
                        W(q, "sy", PEc, wout_rel[wbuf])
                        W(q, "sy", VEc, wout_gate_ve)
                        dstb = wout_sb[:, wbuf * CC * 256:(wbuf + 1) * CC * 256] \
                            .rearrange("p (fc o) -> p fc o", o=256)
                        q.dma_start(dstb, woutT_bf[:, cc * 128:(cc + 2) * 128]
                                    .rearrange("(fc p) o -> p fc o", p=128)).then_inc(SYc.h, 16)
                        SYc.n += 16
                        wout_rdy[cc // 2] = SYc.n

                    blk.sync(wload)

                def wmm(t, cc=cc, wbuf=wbuf, half=half):
                    W(t, "pe", SYc, wout_rdy[cc // 2])
                    W(t, "pe", VEc, c_oh[(half, NH - 1)])
                    if cc < 2:
                        W(t, "pe", VEc)
                        W(t, "pe", SEc)
                    if cc >= 2:
                        W(t, "pe", VEc, c_p1[cc - 2])
                    sub = cc % 2
                    last = None
                    for fc in range(CC):
                        lhs = wout_sb[:, wbuf * CC * 256 + fc * 256 + sub * 128:
                                      wbuf * CC * 256 + fc * 256 + sub * 128 + 128]
                        for s0 in range(0, HW, 512):
                            last = t.matmul(pw[cc % 2][:, s0:s0 + 512], lhs,
                                            oh[:, fc * HW + s0: fc * HW + s0 + 512],
                                            start=(fc == 0), stop=(fc == CC - 1))
                    PEc.inc(last)
                    if cc % 2 == 1:
                        wout_rel[wbuf] = PEc.n

                blk.tensor(wmm)
                c_pw = PEc.n

                def p1f(v, cc=cc, c_pw=c_pw, t0q=t0q):
                    W(v, "ve", PEc, c_pw)
                    c_p1[cc] = vi(v.tensor_add(xb[:, cc * T + t0q: cc * T + t0q + HW],
                                               pw[cc % 2][:, :], xb[:, cc * T + t0q: cc * T + t0q + HW]))

                blk.vector(p1f)

                def sqf(sc, cc=cc, t0q=t0q):
                    W(sc, "se", VEc, c_p1[cc])
                    if cc >= 2:
                        W(sc, "se", PEc, c_stat.get(cc - 2, 0))
                    si(sc.activation(sq_sb[:, (cc % 2) * HW:(cc % 2 + 1) * HW],
                                     xb[:, cc * T + t0q: cc * T + t0q + HW], AF.Square))
                    c_sq[cc] = SEc.n

                blk.scalar(sqf)

                def statf_emit(cc):
                    def statf(t, cc=cc, t0q=t0q):
                        W(t, "pe", VEc, c_p1[cc])
                        W(t, "pe", SEc, c_sq[cc])
                        sl = (cc % 2) * HW
                        st_, sp_ = (cc == 0), (cc == CC - 1)
                        t.matmul(psm[0:1, :], ones_p[:, 0:1],
                                 xb[:, cc * T + t0q: cc * T + t0q + 512], start=st_, stop=sp_)
                        t.matmul(psm2[0:1, :], ones_p[:, 0:1],
                                 xb[:, cc * T + t0q + 512: cc * T + t0q + HW], start=st_, stop=sp_)
                        t.matmul(pss[0:1, :], ones_p[:, 0:1], sq_sb[:, sl:sl + 512], start=st_, stop=sp_)
                        last = t.matmul(pss2[0:1, :], ones_p[:, 0:1], sq_sb[:, sl + 512:sl + HW],
                                        start=st_, stop=sp_)
                        c_stat[cc] = PEc.inc(last)

                    blk.tensor(statf)

                if cc >= 1:
                    statf_emit(cc - 1)
                if cc == CC - 1:
                    statf_emit(cc)

            def rows_math(v):
                W(v, "ve", PEc, c_stat[CC - 1])
                vi(v.tensor_scalar_mul(rowA[0:1, 0:512], psm[0:1, :], 1.0 / D))
                vi(v.tensor_scalar_mul(rowA[0:1, 512:HW], psm2[0:1, :], 1.0 / D))
                vi(v.tensor_scalar_mul(rowB[0:1, 0:512], pss[0:1, :], 1.0 / D))
                b = vi(v.tensor_scalar_mul(rowB[0:1, 512:HW], pss2[0:1, :], 1.0 / D))
                W(v, "ve", VEc, b)
                vi(v.tensor_mul(psm[0:1, 0:512], rowA[0:1, 0:512], rowA[0:1, 0:512]))
                c = vi(v.tensor_mul(psm2[0:1, 0:512], rowA[0:1, 512:HW], rowA[0:1, 512:HW]))
                W(v, "ve", VEc, c)
                vi(v.tensor_sub(rowB[0:1, 0:512], rowB[0:1, 0:512], psm[0:1, 0:512]))
                vi(v.tensor_sub(rowB[0:1, 512:HW], rowB[0:1, 512:HW], psm2[0:1, 0:512]))

            blk.vector(rows_math)
            c_var = VEc.n

            def rows_sqrt(sc):
                W(sc, "se", VEc, c_var)
                W(sc, "se", VEc, const_cnt["cols"])
                si(sc.activation(rowB[0:1, 0:HW], rowB[0:1, 0:HW], AF.Sqrt, bias=ceps[0:1, 0:1]))

            blk.scalar(rows_sqrt)
            c_sqr = SEc.n

            def rows_fin(v):
                W(v, "ve", SEc, c_sqr)
                a = vi(v.reciprocal(rowB[0:1, 0:HW], rowB[0:1, 0:HW]))
                W(v, "ve", VEc, a)
                vi(v.tensor_mul(rowA[0:1, 0:HW], rowA[0:1, 0:HW], rowB[0:1, 0:HW]))

            blk.vector(rows_fin)
            c_rows = VEc.n

            def bcast_ln(t):
                W(t, "pe", VEc, c_rows)
                last = None
                for s0 in range(0, HW, 512):
                    t.matmul(pw[0][:, s0:s0 + 512], ones_rf[0:1, :], rowB[0:1, s0:s0 + 512],
                             start=True, stop=True)
                    last = t.matmul(pw[1][:, s0:s0 + 512], ones_rf[0:1, :], rowA[0:1, s0:s0 + 512],
                                    start=True, stop=True)
                PEc.inc(last)

            blk.tensor(bcast_ln)
            c_bc = PEc.n

            def bc_ev(v, c_bc=c_bc):
                W(v, "ve", PEc, c_bc)
                vi(v.tensor_copy(Abc[:, :], pw[0][:, :]))
                vi(v.tensor_copy(Bbc[:, :], pw[1][:, :]))

            blk.vector(bc_ev)
            c_ab = VEc.n

            for cc in range(CC):
                def y1f(v, cc=cc, t0q=t0q):
                    W(v, "ve", VEc, c_ab)
                    a = vi(v.tensor_mul(t1scr[:, :], xb[:, cc * T + t0q: cc * T + t0q + HW], Abc[:, :]))
                    W(v, "ve", VEc, a)
                    c_y1[cc] = max(c_y1[cc], vi(v.tensor_sub(xb[:, cc * T + t0q: cc * T + t0q + HW],
                                                             t1scr[:, :], Bbc[:, :])))

                blk.vector(y1f)

            es_wp.close()
            es_w.close()

        es_A.close()
        c_y1_all = max(c_y1)

        if "y1" in TAPS:
            def ty1(g):
                W(g, "gp", VEc, c_y1_all)
                g.dma_start(taps["y1"][:, :], xb[:, :]).then_inc(GPc.h, 16)
                GPc.n += 16

            blk.gpsimd(ty1)
        if STOP_AFTER in ("attn", "y1"):
            def fin0(g):
                W(g, "gp", GPc)

            blk.gpsimd(fin0)
            return nc

        es_F = ExitStack()
        w13_sb = es_F.enter_context(nc.sbuf_tensor(nm("w13_sb"), [128, 4 * CC * 256], BF16))
        w2_sb = es_F.enter_context(nc.sbuf_tensor(nm("w2_sb"), [128, 2 * NIC * 256], BF16))
        h_sb = es_F.enter_context(nc.sbuf_tensor(nm("h_sb"), [128, NIC * HW], BF16))
        coef_fm = es_F.enter_context(nc.sbuf_tensor(nm("coef_fm"), [32, HW], BF16))
        crow = es_F.enter_context(nc.sbuf_tensor(nm("crow"), [1, HW], BF16))

        specs = []
        for g in range(4):
            specs.append(("sh", sguT_bf, g * 2 * DI, 0,
                          sdownT_bf[g * DI:(g + 1) * DI, :], 16, "sgu", "sdown"))
        for e in range(NEXP):
            specs.append(("ex", w13T_bf[e], 0, 0, w2T_bf[e], e,
                          f"w13_{e}", f"w2_{e}"))

        ffn_gate_pe = PEc.n
        ffn_gate_ve = VEc.n
        w13_rel = [ffn_gate_pe] * 4
        w13_rdy = {}
        w2_rel = [ffn_gate_pe] * 2
        w2_rdy = {}

        for th in range(2):
            t0 = th * HW

            es_R = ExitStack()
            lg_sb = es_R.enter_context(nc.sbuf_tensor(nm("lg_sb"), [17, HW], F32))
            rtm = es_R.enter_context(nc.sbuf_tensor(nm("rtm"), [128, 8 * 32], F32))
            ctm = es_R.enter_context(nc.sbuf_tensor(nm("ctm"), [128, 8 * 32], F32))
            r8 = es_R.enter_context(nc.sbuf_tensor(nm("r8"), [128, 8], F32))
            rcol = es_R.enter_context(nc.sbuf_tensor(nm("rcol"), [128, 1], F32))
            zapt = es_R.enter_context(nc.sbuf_tensor(nm("zapt"), [128, 16], F32))

            es_Rp = ExitStack()
            plg = es_Rp.enter_context(nc.psum_tensor(nm("plg"), [128, HW], F32))
            ptt = es_Rp.enter_context(nc.psum_tensor(nm("ptt"), [128, 512], F32))

            def rlog(t, t0=t0):
                W(t, "pe", GPc, cast_cnt["gsg"])
                W(t, "pe", VEc, c_y1_all)
                W(t, "pe", VEc)
                W(t, "pe", SEc)
                last = None
                for cc in range(CC):
                    lhs = gsg[:, cc * 32: cc * 32 + 17]
                    for s0 in range(0, HW, 512):
                        last = t.matmul(plg[0:17, s0:s0 + 512], lhs,
                                        xb[:, cc * T + t0 + s0: cc * T + t0 + s0 + 512],
                                        start=(cc == 0), stop=(cc == CC - 1))
                PEc.inc(last)

            blk.tensor(rlog)
            c_lg = PEc.n

            def rlev(sc, c_lg=c_lg):
                W(sc, "se", PEc, c_lg)
                si(sc.activation(lg_sb[:, :], plg[0:17, :], AF.Copy))

            blk.scalar(rlev)
            c_lge = SEc.n

            c_rtm = {}
            for grp in range(2):
                def rtr(t, grp=grp, c_lge=c_lge):
                    W(t, "pe", SEc, c_lge)
                    W(t, "pe", SYc, const_cnt["identf"])
                    if grp == 1:
                        W(t, "pe", VEc, c_rtm[0])
                    last = None
                    for k in range(4):
                        last = t.transpose(ptt[:, k * 128:k * 128 + 17],
                                           lg_sb[:, (grp * 4 + k) * 128:(grp * 4 + k + 1) * 128],
                                           identf[0:17, 0:17])
                    PEc.inc(last)

                blk.tensor(rtr)
                c_t = PEc.n

                def rte(v, grp=grp, c_t=c_t):
                    W(v, "ve", PEc, c_t)
                    src = ptt[:, :].rearrange("p (k o) -> p k o", o=128)[:, :, 0:32]
                    dst = rtm[:, grp * 128:(grp + 1) * 128].rearrange("p (k o) -> p k o", o=32)
                    c_rtm[grp] = vi(v.tensor_copy(dst, src))

                blk.vector(rte)

            c_ctm = {}
            for ch in range(8):
                base = ch * 32

                def soft(v, ch=ch, base=base):
                    W(v, "ve", VEc, c_rtm[ch // 4])
                    c_ctm[("max", ch)] = vi(v.tensor_reduce(rcol[:, 0:1], rtm[:, base:base + 16],
                                                            AX.X, ALU.max, negate=True))

                blk.vector(soft)

                def sexp(sc, ch=ch, base=base):
                    W(sc, "se", VEc, c_ctm[("max", ch)])
                    si(sc.activation(ctm[:, base:base + 16], rtm[:, base:base + 16], AF.Exp,
                                     bias=rcol[:, 0:1]))
                    c_ctm[("exp", ch)] = SEc.n

                blk.scalar(sexp)

                def top4(v, ch=ch, base=base):
                    W(v, "ve", SEc, c_ctm[("exp", ch)])
                    a = vi(v.reduce_sum(rcol[:, 0:1], ctm[:, base:base + 16], AX.X))
                    W(v, "ve", VEc, a)
                    a = vi(v.reciprocal(rcol[:, 0:1], rcol[:, 0:1]))
                    b = vi(v.max(out=r8[:, :], in_=ctm[:, base:base + 16]))
                    W(v, "ve", VEc, b)
                    b = vi(v.memset(r8[:, 4:8], -1.0))
                    W(v, "ve", VEc, b)
                    b = vi(v.match_replace(out=zapt[:, :], in_to_replace=r8[:, :],
                                           in_values=ctm[:, base:base + 16], imm_value=0.0))
                    W(v, "ve", VEc, b)
                    b = vi(v.tensor_sub(zapt[:, :], ctm[:, base:base + 16], zapt[:, :]))
                    W(v, "ve", VEc, b)
                    W(v, "ve", VEc, a)
                    c_ctm[("coef", ch)] = vi(v.tensor_single_scalar(ctm[:, base:base + 16],
                                                                    zapt[:, :], rcol[:, 0:1], ALU.mult))

                blk.vector(top4)

                def sigg(sc, ch=ch, base=base):
                    W(sc, "se", VEc, c_ctm[("coef", ch)])
                    c_ctm[("sig", ch)] = si(sc.activation(ctm[:, base + 16:base + 17],
                                                          rtm[:, base + 16:base + 17], AF.Sigmoid))

                blk.scalar(sigg)

            c_cf = {}
            for grp in range(2):
                def ctr_(t, grp=grp):
                    for ch in range(grp * 4, grp * 4 + 4):
                        W(t, "pe", SEc, c_ctm[("sig", ch)])
                        W(t, "pe", VEc, c_ctm[("coef", ch)])
                    if grp == 1:
                        W(t, "pe", VEc, c_cf[0])
                    last = None
                    for k in range(4):
                        ch = grp * 4 + k
                        last = t.transpose(ptt[0:32, k * 128:(k + 1) * 128],
                                           ctm[:, ch * 32:(ch + 1) * 32], identf[:, :])
                    PEc.inc(last)

                blk.tensor(ctr_)
                c_t2 = PEc.n

                def cfe(v, grp=grp, c_t2=c_t2):
                    W(v, "ve", PEc, c_t2)
                    src = ptt[0:32, :].rearrange("p (k o) -> p k o", o=128)
                    dst = coef_fm[0:32, grp * 512:(grp + 1) * 512].rearrange("p (k o) -> p k o", o=128)
                    c_cf[grp] = vi(v.tensor_copy(dst, src))

                blk.vector(cfe)

            c_coef = VEc.n
            es_Rp.close()
            es_R.close()

            if "coef0" in TAPS and th == 0:
                def tcoef(g):
                    W(g, "gp", VEc, c_coef)
                    g.dma_start(taps["coef0"][:, :], coef_fm[:, :]).then_inc(GPc.h, 16)
                    GPc.n += 16

                blk.gpsimd(tcoef)
            if STOP_AFTER == "router" and th == 0:
                def fin1(g):
                    W(g, "gp", GPc)

                blk.gpsimd(fin1)
                return nc

            es_E = ExitStack()
            silg = es_E.enter_context(nc.sbuf_tensor(nm("silg"), [128, HW], BF16))
            h1s = es_E.enter_context(nc.sbuf_tensor(nm("h1s"), [128, HW], BF16))
            bca = es_E.enter_context(nc.sbuf_tensor(nm("bca"), [128, HW], BF16))

            c_bca = [0]
            c_crow_pe = [0]
            c_pd_last = [0]
            pd_rel = [0, 0]
            c_accL = {}

            c_cr_last = [0]

            def emit_crow(spec_idx):
                ridx = specs[spec_idx][5]

                def gcrow(g):
                    W(g, "gp", VEc, c_coef)
                    W(g, "gp", PEc, c_crow_pe[0])
                    g.dma_start(crow[0:1, :], coef_fm[ridx:ridx + 1, :]).then_inc(GPc.h, 16)
                    GPc.n += 16

                blk.gpsimd(gcrow)
                c_cr_last[0] = GPc.n

            def emit_bcast_mm(spec_idx, pbb):
                c_cr = c_cr_last[0]

                def tbb(t, c_cr=c_cr):
                    W(t, "pe", GPc, c_cr)
                    W(t, "pe", VEc)
                    W(t, "pe", SEc)
                    W(t, "pe", SYc, const_cnt["ones_rb"])
                    last = None
                    for s0 in range(0, HW, 512):
                        last = t.matmul(pbb[:, s0:s0 + 512], ones_rb[0:1, :],
                                        crow[0:1, s0:s0 + 512], start=True, stop=True)
                    PEc.inc(last)
                    c_crow_pe[0] = PEc.n

                blk.tensor(tbb)
                c_pb = PEc.n

                def vbb(v, c_pb=c_pb):
                    W(v, "ve", PEc, c_pb)
                    c_bca[0] = vi(v.tensor_copy(bca[:, :], pbb[:, :]))

                blk.vector(vbb)

            es_bc0 = ExitStack()
            pbb0 = es_bc0.enter_context(nc.psum_tensor(nm("pbb0"), [128, HW], F32))
            emit_crow(0)
            emit_bcast_mm(0, pbb0)
            es_bc0.close()

            specs_run = specs if DEV_N_EXPERTS is None else specs[:DEV_N_EXPERTS]
            for si_, spec in enumerate(specs_run):
                kind, upsrc, gcol, ucol, dnsrc, ridx, tag_up, tag_dn = spec
                cast_pump(4)
                up3 = upsrc.rearrange("(cc p) o -> p cc o", p=128)
                dn3 = dnsrc.rearrange("(ic p) c -> p ic c", p=128)
                my = {}

                if si_ + 1 < len(specs_run):
                    emit_crow(si_ + 1)
                es_up = ExitStack()
                pg = [es_up.enter_context(nc.psum_tensor(nm(f"pg{i}"), [128, HW], F32)) for i in range(2)]
                pu = [es_up.enter_context(nc.psum_tensor(nm(f"pu{i}"), [128, HW], F32)) for i in range(2)]
                for ic in range(NIC):
                    slot = ic % 4

                    def wl(q, slot=slot, ic=ic, up3=up3, c_upcast=cast_cnt[tag_up], si_=si_, gcol=gcol):
                        W(q, "sy", GPc, c_upcast)
                        W(q, "sy", PEc, w13_rel[slot])
                        W(q, "sy", VEc, ffn_gate_ve)
                        dst = w13_sb[:, slot * CC * 256:(slot + 1) * CC * 256] \
                            .rearrange("p (cc o) -> p cc o", o=256)
                        q.dma_start(dst,
                                    up3[:, :, gcol + ic * 256: gcol + (ic + 1) * 256]) \
                            .then_inc(SYc.h, 16)
                        SYc.n += 16
                        w13_rdy[(th, si_, ic)] = SYc.n

                    blk.sync(wl)

                    def upmm(t, ic=ic, slot=slot, si_=si_, t0=t0):
                        W(t, "pe", SYc, w13_rdy[(th, si_, ic)])
                        if ic == 0:
                            W(t, "pe", VEc)
                            W(t, "pe", SEc)
                        if ic >= 2:
                            W(t, "pe", SEc, my.get(("sil", ic - 2), 0))
                            W(t, "pe", VEc, my.get(("h1", ic - 2), 0))
                        lastu = None
                        for cc in range(CC):
                            lg_ = w13_sb[:, slot * CC * 256 + cc * 256: slot * CC * 256 + cc * 256 + 128]
                            lu_ = w13_sb[:, slot * CC * 256 + cc * 256 + 128: slot * CC * 256 + (cc + 1) * 256]
                            for s0 in range(0, HW, 512):
                                rhs = xb[:, cc * T + t0 + s0: cc * T + t0 + s0 + 512]
                                t.matmul(pg[ic % 2][:, s0:s0 + 512], lg_, rhs,
                                         start=(cc == 0), stop=(cc == CC - 1))
                                lastu = t.matmul(pu[ic % 2][:, s0:s0 + 512], lu_, rhs,
                                                 start=(cc == 0), stop=(cc == CC - 1))
                        PEc.inc(lastu)
                        w13_rel[slot] = PEc.n

                    blk.tensor(upmm)
                    c_up = PEc.n

                    def silf(sc, ic=ic, c_up=c_up):
                        W(sc, "se", PEc, c_up)
                        if ic >= 1:
                            W(sc, "se", VEc, my.get(("h", ic - 1), 0))
                        si(sc.activation(silg[:, :], pg[ic % 2][:, :], AF.Silu))
                        my[("sil", ic)] = SEc.n

                    blk.scalar(silf)

                    def h1f(v, ic=ic, c_up=c_up, si_=si_, th=th):
                        W(v, "ve", PEc, c_up)
                        W(v, "ve", VEc, c_bca[0])
                        if si_ > 0 or th > 0:
                            W(v, "ve", PEc, c_pd_last[0])
                        if si_ == 0 and ic == 0:
                            W(v, "ve", SEc)
                        my[("h1", ic)] = vi(v.tensor_mul(h1s[:, :], pu[ic % 2][:, :], bca[:, :]))

                    blk.vector(h1f)

                    def hf(v, ic=ic):
                        W(v, "ve", SEc, my[("sil", ic)])
                        W(v, "ve", VEc, my[("h1", ic)])
                        my[("h", ic)] = vi(v.tensor_mul(h_sb[:, ic * HW:(ic + 1) * HW],
                                                        h1s[:, :], silg[:, :]))

                    blk.vector(hf)

                c_hall = VEc.n
                es_up.close()

                es_dn = ExitStack()
                pd = [es_dn.enter_context(nc.psum_tensor(nm(f"pd{i}"), [128, HW], F32)) for i in range(2)]
                if si_ + 1 < len(specs_run):
                    es_bc = ExitStack()
                    pbb = es_bc.enter_context(nc.psum_tensor(nm("pbb"), [128, HW], F32))
                    emit_bcast_mm(si_ + 1, pbb)
                    es_bc.close()
                for cw in range(8):
                    slot = cw % 2

                    def w2l(q, cw=cw, slot=slot, dn3=dn3, c_dncast=cast_cnt[tag_dn], si_=si_):
                        W(q, "sy", GPc, c_dncast)
                        W(q, "sy", PEc, w2_rel[slot])
                        W(q, "sy", VEc, ffn_gate_ve)
                        dst = w2_sb[:, slot * NIC * 256:(slot + 1) * NIC * 256] \
                            .rearrange("p (ic o) -> p ic o", o=256)
                        q.dma_start(dst, dn3[:, :, cw * 256:(cw + 1) * 256]).then_inc(SYc.h, 16)
                        SYc.n += 16
                        w2_rdy[(th, si_, cw)] = SYc.n

                    blk.sync(w2l)

                    for ccl in range(2):
                        cc = cw * 2 + ccl
                        pbuf = cc % 2

                        def dmm(t, cc=cc, ccl=ccl, cw=cw, slot=slot, pbuf=pbuf, si_=si_, c_hall=c_hall):
                            W(t, "pe", SYc, w2_rdy[(th, si_, cw)])
                            W(t, "pe", VEc, c_hall)
                            if si_ == 0 and cc < 2:
                                W(t, "pe", SEc)
                                W(t, "pe", VEc)
                            W(t, "pe", VEc, pd_rel[pbuf])
                            last = None
                            for ic in range(NIC):
                                lhs = w2_sb[:, slot * NIC * 256 + ic * 256 + ccl * 128:
                                            slot * NIC * 256 + ic * 256 + ccl * 128 + 128]
                                for s0 in range(0, HW, 512):
                                    last = t.matmul(pd[pbuf][:, s0:s0 + 512], lhs,
                                                    h_sb[:, ic * HW + s0: ic * HW + s0 + 512],
                                                    start=(ic == 0), stop=(ic == NIC - 1))
                            PEc.inc(last)
                            w2_rel[slot] = PEc.n
                            c_pd_last[0] = PEc.n

                        blk.tensor(dmm)
                        c_pd = PEc.n

                        def accf(v, cc=cc, pbuf=pbuf, si_=si_, c_pd=c_pd):
                            W(v, "ve", PEc, c_pd)
                            if si_ == 0:
                                a = vi(v.tensor_copy(oh[:, cc * HW:(cc + 1) * HW], pd[pbuf][:, :]))
                            else:
                                a = vi(v.tensor_add(oh[:, cc * HW:(cc + 1) * HW],
                                                    oh[:, cc * HW:(cc + 1) * HW], pd[pbuf][:, :]))
                            pd_rel[pbuf] = a
                            c_accL[cc] = a

                        blk.vector(accf)
                es_dn.close()

                if "acc0" in TAPS and th == 0 and si_ == 0:
                    def tacc0(g):
                        W(g, "gp", VEc, max(c_accL.values()))
                        g.dma_start(taps["acc0"][:, :], oh[:, :]).then_inc(GPc.h, 16)
                        GPc.n += 16

                    blk.gpsimd(tacc0)
                    c_tap = GPc.n

                    def vwait(v, c_tap=c_tap):
                        W(v, "ve", GPc, c_tap)

                    blk.vector(vwait)

            if "accA" in TAPS and th == 0:
                def taccA(g):
                    W(g, "gp", VEc, max(c_accL.values()))
                    g.dma_start(taps["accA"][:, :], oh[:, :]).then_inc(GPc.h, 16)
                    GPc.n += 16

                blk.gpsimd(taccA)
            es_E.close()
            if STOP_AFTER == "ffnA" and th == 0:
                def fin2(g):
                    W(g, "gp", GPc)

                blk.gpsimd(fin2)
                return nc

            es_L = ExitStack()
            y2row2 = es_L.enter_context(nc.sbuf_tensor(nm("y2row"), [128, 2 * D], F32))
            p2row = h_sb[:, 0:CC * 512]
            p2tm = h_sb[:, CC * 512:CC * 512 + D]
            scrq = h_sb[:, CC * 512 + D:CC * 512 + D + 512]

            es_Lp = ExitStack()
            ptb2 = [es_Lp.enter_context(nc.psum_tensor(nm(f"ptb2_{i}"), [128, 512], BF16)) for i in range(2)]

            c_y2dma = [GPc.n, GPc.n]
            for grp in range(2):
                c_p2r = {}
                for cc in range(CC):
                    def p2f(v, cc=cc, grp=grp, t0=t0):
                        W(v, "ve", VEc, c_accL[cc])
                        W(v, "ve", PEc, c_pd_last[0])
                        c_p2r[cc] = vi(v.tensor_add(p2row[:, cc * 512:(cc + 1) * 512],
                                                    xb[:, cc * T + t0 + grp * 512: cc * T + t0 + grp * 512 + 512],
                                                    oh[:, cc * HW + grp * 512: cc * HW + grp * 512 + 512]))

                    blk.vector(p2f)

                c_ev4 = {}
                for tch in range(4):
                    for b4 in range(4):
                        def p2tr(t, tch=tch, b4=b4, grp=grp):
                            for cc in range(b4 * 4, b4 * 4 + 4):
                                W(t, "pe", VEc, c_p2r[cc])
                            W(t, "pe", SYc, const_cnt["identb"])
                            if (tch * 4 + b4) >= 2:
                                W(t, "pe", VEc, c_ev4[(tch * 4 + b4) - 2])
                            else:
                                W(t, "pe", VEc)
                                W(t, "pe", SEc)
                            last = None
                            for cc in range(b4 * 4, b4 * 4 + 4):
                                last = t.transpose(ptb2[b4 % 2][:, (cc % 4) * 128:((cc % 4) + 1) * 128],
                                                   p2row[:, cc * 512 + tch * 128: cc * 512 + (tch + 1) * 128],
                                                   identb[:, :])
                            PEc.inc(last)

                        blk.tensor(p2tr)
                        c_ptr = PEc.n

                        def p2ev(v, tch=tch, b4=b4, c_ptr=c_ptr):
                            W(v, "ve", PEc, c_ptr)
                            c_ev4[tch * 4 + b4] = vi(v.tensor_copy(p2tm[:, b4 * 512:(b4 + 1) * 512],
                                                                   ptb2[b4 % 2][:, :]))

                        blk.vector(p2ev)

                    c_tm = VEc.n

                    def ln2a(v, c_tm=c_tm):
                        W(v, "ve", VEc, c_tm)
                        vi(v.reduce_sum(mcol[:, 0:1], p2tm[:, :], AX.X))

                    blk.vector(ln2a)
                    c_mc = VEc.n

                    def ln2sq(sc, c_tm=c_tm):
                        W(sc, "se", VEc, c_tm)
                        for k in range(4):
                            si(sc.activation(scrq[:, :], p2tm[:, k * 512:(k + 1) * 512], AF.Square,
                                             accum_out=sscol[:, k:k + 1]))

                    blk.scalar(ln2sq)
                    c_ss = SEc.n

                    def ln2b(v, c_mc=c_mc, c_ss=c_ss):
                        W(v, "ve", VEc, c_mc)
                        W(v, "ve", SEc, c_ss)
                        a = vi(v.reduce_sum(tmpcol[:, 0:1], sscol[:, :], AX.X))
                        W(v, "ve", VEc, a)
                        a = vi(v.tensor_scalar_mul(mcol[:, 0:1], mcol[:, 0:1], 1.0 / D))
                        a = vi(v.tensor_scalar_mul(tmpcol[:, 0:1], tmpcol[:, 0:1], 1.0 / D))
                        W(v, "ve", VEc, a)
                        a = vi(v.tensor_mul(rstdc[:, 0:1], mcol[:, 0:1], mcol[:, 0:1]))
                        W(v, "ve", VEc, a)
                        a = vi(v.tensor_sub(rstdc[:, 0:1], tmpcol[:, 0:1], rstdc[:, 0:1]))

                    blk.vector(ln2b)
                    c_v1 = VEc.n

                    def ln2c(sc, c_v1=c_v1):
                        W(sc, "se", VEc, c_v1)
                        W(sc, "se", VEc, const_cnt["cols"])
                        si(sc.activation(rstdc[:, 0:1], rstdc[:, 0:1], AF.Sqrt, bias=ceps[:, 0:1]))

                    blk.scalar(ln2c)
                    c_v2 = SEc.n

                    def ln2d(v, c_v2=c_v2):
                        W(v, "ve", SEc, c_v2)
                        a = vi(v.reciprocal(rstdc[:, 0:1], rstdc[:, 0:1]))
                        W(v, "ve", VEc, a)
                        a = vi(v.scalar_tensor_tensor(bcol[:, 0:1], mcol[:, 0:1], -1.0,
                                                      rstdc[:, 0:1], ALU.mult, ALU.mult))
                        W(v, "ve", VEc, a)
                        W(v, "ve", GPc, c_y2dma[(grp * 4 + tch) % 2])
                        ysl = ((grp * 4 + tch) % 2) * D
                        vi(v.tensor_scalar(y2row2[:, ysl:ysl + D], p2tm[:, :], rstdc[:, 0:1],
                                           bcol[:, 0:1], ALU.mult, ALU.add))

                    blk.vector(ln2d)
                    c_y2 = VEc.n

                    def outd(g, grp=grp, tch=tch, t0=t0, c_y2=c_y2):
                        W(g, "gp", VEc, c_y2)
                        r0 = t0 + grp * 512 + tch * 128
                        ysl = ((grp * 4 + tch) % 2) * D
                        g.dma_start(y_out[r0:r0 + 128, :], y2row2[:, ysl:ysl + D]).then_inc(GPc.h, 16)
                        GPc.n += 16
                        c_y2dma[(grp * 4 + tch) % 2] = GPc.n

                    blk.gpsimd(outd)

            es_Lp.close()
            es_L.close()

        def fin(g):
            W(g, "gp", GPc)

        blk.gpsimd(fin)

    assert not cast_queue, "cast queue not fully pumped"
    return nc


def _prep_inputs(inputs):
    asc = np.ascontiguousarray
    f32 = np.float32
    wqkv = np.asarray(inputs["L0_attn_w_qkv"], dtype=f32)
    wout = np.asarray(inputs["L0_attn_w_out"], dtype=f32)
    gate = np.asarray(inputs["L0_ffn_gate_w"], dtype=f32)
    sgate = np.asarray(inputs["L0_ffn_sgate_w"], dtype=f32)
    w13 = np.asarray(inputs["L0_ffn_w13"], dtype=f32)
    w2 = np.asarray(inputs["L0_ffn_w2"], dtype=f32)
    sgu = np.asarray(inputs["L0_ffn_shared_gate_up_w"], dtype=f32)
    sdown = np.asarray(inputs["L0_ffn_shared_down_w"], dtype=f32)
    gsgT = np.zeros((D, 32), dtype=f32)
    gsgT[:, 0:16] = gate.T
    gsgT[:, 16] = sgate[0]
    def il_w13(wt):
        g = wt[:, 0:DI].reshape(D, NIC, 128)
        u = wt[:, DI:2 * DI].reshape(D, NIC, 128)
        return np.stack([g, u], axis=2).reshape(D, 2 * DI)

    sguT_ = sgu.T
    sgu_groups = []
    for gi in range(4):
        gg = sguT_[:, gi * DI:(gi + 1) * DI].reshape(D, NIC, 128)
        uu = sguT_[:, DIS + gi * DI:DIS + (gi + 1) * DI].reshape(D, NIC, 128)
        sgu_groups.append(np.stack([gg, uu], axis=2).reshape(D, 2 * DI))
    sguT_il = asc(np.concatenate(sgu_groups, axis=1))
    w13T_il = asc(np.stack([il_w13(np.asarray(w13[e].T)) for e in range(NEXP)], axis=0))
    base = {
        "wqkvT": asc(wqkv.T),
        "woutT": asc(wout.T),
        "gsgT": gsgT,
        "sguT": sguT_il,
        "sdownT": asc(sdown.T),
        "w13T": w13T_il,
        "w2T": asc(w2.transpose(0, 2, 1)),
    }
    x = np.asarray(inputs["L0_attn_x"], dtype=f32)
    return base, x


def kernel(**inputs):
    base, x = _prep_inputs(inputs)
    nc = build()
    in_maps = []
    for b in range(N_CORES):
        m = dict(base)
        m["xT"] = np.ascontiguousarray(x[b].T)
        in_maps.append(m)
    res = run_bass_kernel_spmd(nc, in_maps, core_ids=list(range(N_CORES)))
    return np.stack([r["y_out"] for r in res.results], axis=0)
